# version 02 (final)
# baseline (speedup 1.0000x reference)
import contextlib

import numpy as np
import ml_dtypes
import concourse.bass as bass
import concourse.mybir as mybir
from concourse.bass_utils import run_bass_kernel_spmd

F32 = mybir.dt.float32
BF16 = mybir.dt.bfloat16
AF = mybir.ActivationFunctionType
ALU = mybir.AluOpType
AX = mybir.AxisListType

NCORES = 8
D = 1024
KT = 8
T = 2048
B = 2
SEQ = 8192
NB = SEQ // T
BF = ml_dtypes.bfloat16
TCW = 512
TC = T // TCW

NEG = -1.0e30
HID = 2816
MT1 = HID // 128


class Em:

    def __init__(self, nc, block, stack):
        self.nc = nc
        self.block = block
        self.stack = stack
        self.sems = {}
        self.cnt = {}

    def sem(self, name):
        if name not in self.sems:
            self.sems[name] = self.stack.enter_context(self.nc.semaphore(name))
            self.cnt[name] = 0
        return self.sems[name]

    def wg(self, eng, name, val=None):
        s = self.sem(name)
        v = self.cnt[name] if val is None else val
        if v > 0:
            eng.wait_ge(s, v)

    def inc(self, ins, name, amt=1):
        s = self.sem(name)
        self.cnt[name] += amt
        ins.then_inc(s, amt)
        return self.cnt[name]

    def sync(self, fn):
        self.block.sync(fn)

    def scalar(self, fn):
        self.block.scalar(fn)

    def vector(self, fn):
        self.block.vector(fn)

    def tensor(self, fn):
        self.block.tensor(fn)

    def gpsimd(self, fn):
        self.block.gpsimd(fn)


def sbt(nc, stack, name, shape, dtype):
    return stack.enter_context(nc.sbuf_tensor(name, list(shape), dtype))


def pst(nc, stack, name, shape, dtype=F32):
    return stack.enter_context(nc.psum_tensor(name, list(shape), dtype))


def kpt(ap):
    return ap.rearrange("(k p) t -> p k t", p=128)


def to_fm(x):
    out = []
    for b in range(B):
        xb = np.ascontiguousarray(np.asarray(x[b], np.float32).T)
        for s in range(NB):
            out.append(np.ascontiguousarray(xb[:, s * T:(s + 1) * T]))
    return out


def from_fm(shards):
    x = np.empty((B, SEQ, D), np.float32)
    for c, sh in enumerate(shards):
        b, s = divmod(c, NB)
        x[b, s * T:(s + 1) * T, :] = np.asarray(sh, np.float32).T
    return x


def ext_slices(u_cores, halo):
    out = []
    for c, u in enumerate(u_cores):
        b, s = divmod(c, NB)
        if s == 0:
            h = np.zeros((D, halo), np.float32)
        else:
            h = np.asarray(u_cores[c - 1], np.float32)[:, T - halo:]
        out.append(np.ascontiguousarray(
            np.concatenate([h, np.asarray(u, np.float32)], axis=1)))
    return out


def per_part(vec, kt=None):
    v = np.asarray(vec, np.float32).ravel()
    kt = len(v) // 128 if kt is None else kt
    return np.ascontiguousarray(v.reshape(kt, 128).T)


ONES_COL = np.ones((128, 1), np.float32)
ONES_ROW = np.ones((1, 128), np.float32)
EPS5 = np.full((1, 1), 1e-5, np.float32)


def _stats_and_norm(P, nc, tc, src_view, p_s, p_q, sqf, onc, onr, epst, srow,
                    p_mb, p_rb, ln_eps_ap):
    def squares(s_):
        if tc >= 2:
            P.wg(s_, "MQ", tc - 1)
        if tc >= 1:
            P.wg(s_, "VU", tc)
        for k in range(KT):
            ins = s_.activation(sqf[:, k * TCW:(k + 1) * TCW], src_view(k), AF.Square)
            if k == KT - 1:
                P.inc(ins, "SQ")
    P.scalar(squares)

    def stats_mm(t):
        if tc >= 1:
            P.wg(t, "VM", 3 * tc)
        for k in range(KT):
            ins = t.matmul(p_s[:, :], onc[:, :], src_view(k),
                           start=(k == 0), stop=(k == KT - 1))
            if k == KT - 1:
                P.inc(ins, "MS")
        P.wg(t, "SQ", P.cnt["SQ"])
        for k in range(KT):
            ins = t.matmul(p_q[:, :], onc[:, :], sqf[:, k * TCW:(k + 1) * TCW],
                           start=(k == 0), stop=(k == KT - 1))
            if k == KT - 1:
                P.inc(ins, "MQ")
    P.tensor(stats_mm)

    mu = srow[:1, 0:TCW]
    va = srow[:1, TCW:2 * TCW]

    def stats_math(v):
        P.wg(v, "MS", P.cnt["MS"])
        P.wg(v, "MQ", P.cnt["MQ"])
        P.inc(v.tensor_scalar_mul(mu, p_s[:1, :], 1.0 / D), "VM")
        P.wg(v, "VM")
        P.inc(v.tensor_mul(va, mu, mu), "VM")
        P.wg(v, "VM")
        P.inc(v.scalar_tensor_tensor(va, p_q[:1, :], 1.0 / D, va,
                                     ALU.mult, ALU.subtract), "VM")
    P.vector(stats_math)

    def stats_sqrt(s_):
        P.wg(s_, "VM")
        P.inc(s_.activation(va, va, AF.Sqrt, bias=ln_eps_ap), "SS")
    P.scalar(stats_sqrt)

    def stats_recip(v):
        P.wg(v, "SS")
        P.inc(v.reciprocal(va, va), "VR")
    P.vector(stats_recip)

    def bcast(t):
        P.wg(t, "VR")
        if tc >= 1:
            P.wg(t, "VU")
        t.matmul(p_mb[:, :], onr[:, :], mu, start=True, stop=True)
        P.inc(t.matmul(p_rb[:, :], onr[:, :], va, start=True, stop=True), "MB")
    P.tensor(bcast)


_prog_cache = {}


def build_pln():
    if "pln" in _prog_cache:
        return _prog_cache["pln"]
    nc = bass.Bass("TRN2", target_bir_lowering=False, debug=False)
    stack = contextlib.ExitStack()
    with stack:
        xin = nc.dram_tensor("xin", [D, T], F32, kind="ExternalInput").ap()
        onc_d = nc.dram_tensor("onc", [128, 1], F32, kind="ExternalInput").ap()
        onr_d = nc.dram_tensor("onr", [1, 128], F32, kind="ExternalInput").ap()
        eps_d = nc.dram_tensor("eps", [1, 1], F32, kind="ExternalInput").ap()
        uout = nc.dram_tensor("uout", [D, T], F32, kind="ExternalOutput").ap()

        xt = [sbt(nc, stack, f"xt{i}", [128, KT * TCW], F32) for i in range(2)]
        sqf = [sbt(nc, stack, f"sqf{i}", [128, KT * TCW], F32) for i in range(2)]
        uo = [sbt(nc, stack, f"uo{i}", [128, KT * TCW], F32) for i in range(2)]
        onc = sbt(nc, stack, "onc_s", [128, 1], F32)
        onr = sbt(nc, stack, "onr_s", [1, 128], F32)
        epst = sbt(nc, stack, "epst_s", [1, 1], F32)
        srow = sbt(nc, stack, "srow", [1, 2 * TCW], F32)
        p_s = pst(nc, stack, "p_s", [1, TCW])
        p_q = pst(nc, stack, "p_q", [1, TCW])
        p_mb = pst(nc, stack, "p_mb", [128, TCW])
        p_rb = pst(nc, stack, "p_rb", [128, TCW])

        with nc.Block() as block:
            P = Em(nc, block, stack)

            def consts(q):
                P.inc(q.dma_start(out=onc[:, :], in_=onc_d[:, :]), "Lc", 16)
                P.inc(q.dma_start(out=onr[:, :], in_=onr_d[:, :]), "Lc", 16)
                P.inc(q.dma_start(out=epst[:, :], in_=eps_d[:, :]), "Lc", 16)
            P.sync(consts)

            for tc in range(TC):
                i = tc % 2

                def load(q, tc=tc, i=i):
                    if tc >= 2:
                        P.wg(q, "VU", tc - 1)
                    P.inc(q.dma_start(
                        out=xt[i].rearrange("p (k t) -> p k t", k=KT),
                        in_=kpt(xin)[:, :, tc * TCW:(tc + 1) * TCW]), f"Lx{i}", 16)
                P.sync(load)

                def wload(e_, tc=tc, i=i):
                    P.wg(e_, f"Lx{i}", 16 * (tc // 2 + 1))
                    if tc == 0:
                        P.wg(e_, "Lc", 48)
                P.scalar(wload)
                P.tensor(wload)

                _stats_and_norm(P, nc, tc, lambda k, i=i: xt[i][:, k * TCW:(k + 1) * TCW],
                                p_s, p_q, sqf[i], onc, onr, epst, srow, p_mb, p_rb,
                                epst[:1, 0:1])

                def norm(v, tc=tc, i=i):
                    P.wg(v, "MB")
                    if tc >= 2:
                        P.wg(v, f"SO{i}", 16 * (tc // 2))
                    for k in range(KT):
                        sl = uo[i][:, k * TCW:(k + 1) * TCW]
                        v.tensor_sub(sl, xt[i][:, k * TCW:(k + 1) * TCW], p_mb[:, :])
                        ins = v.tensor_mul(sl, sl, p_rb[:, :])
                        if k == KT - 1:
                            P.inc(ins, "VU")
                P.vector(norm)

                def store(g, tc=tc, i=i):
                    P.wg(g, "VU", tc + 1)
                    P.inc(g.dma_start(
                        out=kpt(uout)[:, :, tc * TCW:(tc + 1) * TCW],
                        in_=uo[i].rearrange("p (k t) -> p k t", k=KT)), f"SO{i}", 16)
                P.gpsimd(store)

            P.gpsimd(lambda g: (P.wg(g, "SO0"), P.wg(g, "SO1")))
        _prog_cache["pln"] = nc
    return nc


def run_pln(x_cores):
    nc = build_pln()
    in_maps = [dict(xin=np.ascontiguousarray(x), onc=ONES_COL, onr=ONES_ROW,
                    eps=EPS5) for x in x_cores]
    res = run_bass_kernel_spmd(nc, in_maps, core_ids=list(range(NCORES)))
    return [r["uout"] for r in res.results]


THW = 1024
NTH = T // THW


def build_pf(last=False):
    key = f"pf{int(last)}"
    if key in _prog_cache:
        return _prog_cache[key]
    nc = bass.Bass("TRN2", target_bir_lowering=False, debug=False)
    stack = contextlib.ExitStack()
    with stack:
        uin = nc.dram_tensor("uin", [D, T + 1], F32, kind="ExternalInput").ap()
        w1 = nc.dram_tensor("w1", [D, 2 * HID], F32, kind="ExternalInput").ap()
        w2 = nc.dram_tensor("w2", [HID, D], F32, kind="ExternalInput").ap()
        lng_d = nc.dram_tensor("lng", [128, KT], F32, kind="ExternalInput").ap()
        lnb_d = nc.dram_tensor("lnb", [128, KT], F32, kind="ExternalInput").ap()
        b1a_d = nc.dram_tensor("b1a", [128, MT1], F32, kind="ExternalInput").ap()
        b1g_d = nc.dram_tensor("b1g", [128, MT1], F32, kind="ExternalInput").ap()
        b2_d = nc.dram_tensor("b2", [128, KT], F32, kind="ExternalInput").ap()
        onc_d = nc.dram_tensor("onc", [128, 1], F32, kind="ExternalInput").ap()
        onr_d = nc.dram_tensor("onr", [1, 128], F32, kind="ExternalInput").ap()
        eps_d = nc.dram_tensor("eps", [1, 1], F32, kind="ExternalInput").ap()
        if last:
            xres = nc.dram_tensor("xres", [D, T], F32, kind="ExternalInput").ap()
        uout = nc.dram_tensor("uout", [D, T], F32, kind="ExternalOutput").ap()

        un = sbt(nc, stack, "un", [128, KT * T], BF16)
        zt = sbt(nc, stack, "zt", [128, MT1 * THW], BF16)
        w2r = sbt(nc, stack, "w2r", [128, MT1 * D], BF16)
        ust = sbt(nc, stack, "ust", [128, KT * (TCW + 1)], F32)
        sqf = sbt(nc, stack, "sqf", [128, KT * TCW], F32)
        w1s = [sbt(nc, stack, f"w1s{i}", [128, KT * 256], BF16) for i in range(2)]
        urs = [sbt(nc, stack, f"urs{i}", [128, TCW], F32) for i in range(2)]
        xrs = [sbt(nc, stack, f"xrs{i}", [128, TCW], F32) for i in range(2)] if last else None
        gel = [sbt(nc, stack, f"gel{i}", [128, TCW], F32) for i in range(2)]
        uo = [sbt(nc, stack, f"uo{i}", [128, TCW], F32) for i in range(2)]
        prm = sbt(nc, stack, "prm", [128, 3 * KT], F32)
        b1t = sbt(nc, stack, "b1t", [128, 2 * MT1], F32)
        onc = sbt(nc, stack, "onc_s", [128, 1], F32)
        onr = sbt(nc, stack, "onr_s", [1, 128], F32)
        epst = sbt(nc, stack, "epst_s", [1, 1], F32)
        srow = sbt(nc, stack, "srow", [1, 2 * TCW], F32)
        p_s = pst(nc, stack, "p_s", [1, TCW])
        p_q = pst(nc, stack, "p_q", [1, TCW])
        p_mb = pst(nc, stack, "p_mb", [128, TCW])
        p_rb = pst(nc, stack, "p_rb", [128, TCW])
        p_ha = [pst(nc, stack, f"p_ha{i}", [128, TCW]) for i in range(2)]
        p_hg = [pst(nc, stack, f"p_hg{i}", [128, TCW]) for i in range(2)]

        CW = TCW + 1

        def shview(buf, k, w=TCW):
            off = 1 if k < 4 else 0
            return buf[:, k * CW + off: k * CW + off + w]

        with nc.Block() as block:
            P = Em(nc, block, stack)

            def consts(q):
                P.inc(q.dma_start(out=prm[:, 0:KT], in_=lng_d[:, :]), "Lc", 16)
                P.inc(q.dma_start(out=prm[:, KT:2 * KT], in_=lnb_d[:, :]), "Lc", 16)
                P.inc(q.dma_start(out=prm[:, 2 * KT:3 * KT], in_=b2_d[:, :]), "Lc", 16)
                P.inc(q.dma_start(out=b1t[:, 0:MT1], in_=b1a_d[:, :]), "Lc", 16)
                P.inc(q.dma_start(out=b1t[:, MT1:2 * MT1], in_=b1g_d[:, :]), "Lc", 16)
                P.inc(q.dma_start(out=onc[:, :], in_=onc_d[:, :]), "Lc", 16)
                P.inc(q.dma_start(out=onr[:, :], in_=onr_d[:, :]), "Lc", 16)
                P.inc(q.dma_start(out=epst[:, :], in_=eps_d[:, :]), "Lc", 16)
            P.sync(consts)
            n_const = P.cnt["Lc"]

            def w2load(g):
                P.inc(g.dma_start(out=w2r.rearrange("p (k m) -> p k m", k=MT1),
                                  in_=w2.rearrange("(k p) m -> p k m", p=128)), "Lw2", 16)
            P.gpsimd(w2load)

            for tc in range(TC):
                def load(q, tc=tc):
                    if tc >= 1:
                        P.wg(q, "VU", tc)
                    P.inc(q.dma_start(
                        out=ust.rearrange("p (k t) -> p k t", k=KT),
                        in_=kpt(uin)[:, :, tc * TCW: tc * TCW + CW]), "Lu", 16)
                P.sync(load)

                def wload(e_, tc=tc):
                    P.wg(e_, "Lu", 16 * (tc + 1))
                    if tc == 0:
                        P.wg(e_, "Lc", n_const)
                P.scalar(wload)
                P.tensor(wload)

                _stats_and_norm(P, nc, tc, lambda k: shview(ust, k),
                                p_s, p_q, sqf, onc, onr, epst, srow,
                                p_mb, p_rb, epst[:1, 0:1])

                def norm(v, tc=tc):
                    P.wg(v, "MB")
                    P.wg(v, "MQ")
                    for k in range(KT):
                        sl = sqf[:, k * TCW:(k + 1) * TCW]
                        v.tensor_sub(sl, shview(ust, k), p_mb[:, :])
                        P.inc(v.tensor_mul(sl, sl, p_rb[:, :]), "VUe")
                        P.wg(v, "VUe")
                        ins = v.tensor_scalar(
                            un[:, k * T + tc * TCW: k * T + (tc + 1) * TCW],
                            sl, prm[:, k:k + 1], prm[:, KT + k:KT + k + 1],
                            ALU.mult, ALU.add)
                        if k == KT - 1:
                            P.inc(ins, "VU")
                P.vector(norm)

            n2 = 0
            n3 = 0
            for th in range(NTH):
                for mp in range(MT1):
                    i = mp % 2

                    def w1load(g, th=th, mp=mp, i=i):
                        gidx = th * MT1 + mp
                        if gidx >= 2:
                            P.wg(g, "HMM", 2 * (gidx - 1))
                        v = w1s[i].rearrange("p (k m) -> p k m", k=KT)
                        P.inc(g.dma_start(
                            out=v[:, :, 0:128],
                            in_=w1.rearrange("(k p) m -> p k m", p=128)[:, :, mp * 128:(mp + 1) * 128]),
                            f"Lw1{i}", 16)
                        P.inc(g.dma_start(
                            out=v[:, :, 128:256],
                            in_=w1.rearrange("(k p) m -> p k m", p=128)[:, :, HID + mp * 128: HID + (mp + 1) * 128]),
                            f"Lw1{i}", 16)
                    P.gpsimd(w1load)

                    for tc2 in range(2):
                        j = n2 % 2
                        tof = th * THW + tc2 * TCW

                        def h_mm(t, th=th, mp=mp, tc2=tc2, i=i, j=j, n2=n2, tof=tof):
                            if mp == 0 and tc2 == 0:
                                if th == 0:
                                    P.wg(t, "VU", TC)
                                else:
                                    P.wg(t, "EV", P.cnt["EV"])
                            P.wg(t, f"Lw1{i}", 32 * ((th * MT1 + mp) // 2 + 1))
                            if n2 >= 2:
                                P.wg(t, "ZD", n2 - 1)
                            wv = w1s[i].rearrange("p (k m) -> p k m", k=KT)
                            for k in range(KT):
                                rhs = un[:, k * T + tof: k * T + tof + TCW]
                                t.matmul(p_ha[j][:, :], wv[:, k, 0:128], rhs,
                                         start=(k == 0), stop=(k == KT - 1))
                            for k in range(KT):
                                rhs = un[:, k * T + tof: k * T + tof + TCW]
                                ins = t.matmul(p_hg[j][:, :], wv[:, k, 128:256], rhs,
                                               start=(k == 0), stop=(k == KT - 1))
                                if k == KT - 1:
                                    P.inc(ins, "HMM")
                        P.tensor(h_mm)

                        def gelu(s_, mp=mp, j=j, n2=n2):
                            P.wg(s_, "HMM", P.cnt["HMM"])
                            if n2 >= 2:
                                P.wg(s_, "ZD", n2 - 1)
                            P.inc(s_.activation(gel[j][:, :], p_hg[j][:, :], AF.Gelu,
                                                bias=b1t[:, MT1 + mp:MT1 + mp + 1]), "GE")
                        P.scalar(gelu)

                        def zpass(v, mp=mp, tc2=tc2, j=j):
                            P.wg(v, "GE", P.cnt["GE"])
                            P.inc(v.scalar_tensor_tensor(
                                zt[:, mp * THW + tc2 * TCW: mp * THW + (tc2 + 1) * TCW],
                                p_ha[j][:, :], b1t[:, mp:mp + 1], gel[j][:, :],
                                ALU.add, ALU.mult), "ZD")
                        P.vector(zpass)
                        n2 += 1

                for mo in range(KT):
                    for tc2 in range(2):
                        j = n3 % 2
                        tof = th * THW + tc2 * TCW

                        def ures(q, mo=mo, j=j, n3=n3, tof=tof):
                            if n3 >= 2:
                                P.wg(q, "EV", n3 - 1)
                            P.inc(q.dma_start(
                                out=urs[j][:, :],
                                in_=uin[mo * 128:(mo + 1) * 128, 1 + tof: 1 + tof + TCW]),
                                f"Lur{j}", 16)
                            if last:
                                P.inc(q.dma_start(
                                    out=xrs[j][:, :],
                                    in_=xres[mo * 128:(mo + 1) * 128, tof: tof + TCW]),
                                    f"Lur{j}", 16)
                        P.sync(ures)

                        def o_mm(t, mo=mo, tc2=tc2, j=j, n3=n3, th=th):
                            if mo == 0 and tc2 == 0:
                                P.wg(t, "ZD", P.cnt["ZD"])
                                if th == 0:
                                    P.wg(t, "Lw2", 16)
                            if n3 >= 2:
                                P.wg(t, "EV", n3 - 1)
                            wv = w2r.rearrange("p (k m) -> p k m", k=MT1)
                            for k in range(MT1):
                                ins = t.matmul(p_ha[j][:, :], wv[:, k, mo * 128:(mo + 1) * 128],
                                               zt[:, k * THW + tc2 * TCW: k * THW + (tc2 + 1) * TCW],
                                               start=(k == 0), stop=(k == MT1 - 1))
                                if k == MT1 - 1:
                                    P.inc(ins, "OMM")
                        P.tensor(o_mm)

                        def evac(v, mo=mo, j=j, n3=n3):
                            P.wg(v, "OMM", P.cnt["OMM"])
                            P.wg(v, f"Lur{j}", P.cnt[f"Lur{j}"])
                            if n3 >= 2:
                                P.wg(v, f"SO{j}", 16 * (n3 // 2))
                            ins = v.scalar_tensor_tensor(
                                uo[j][:, :], p_ha[j][:, :], prm[:, 2 * KT + mo: 2 * KT + mo + 1],
                                urs[j][:, :], ALU.add, ALU.add)
                            if last:
                                P.inc(ins, "EVa")
                                P.wg(v, "EVa")
                                ins = v.scalar_tensor_tensor(
                                    uo[j][:, :], uo[j][:, :], 0.5, xrs[j][:, :],
                                    ALU.mult, ALU.add)
                            P.inc(ins, "EV")
                        P.vector(evac)

                        def store(g, mo=mo, j=j, tof=tof):
                            P.wg(g, "EV", P.cnt["EV"])
                            P.inc(g.dma_start(
                                out=uout[mo * 128:(mo + 1) * 128, tof: tof + TCW],
                                in_=uo[j][:, :]), f"SO{j}", 16)
                        P.gpsimd(store)
                        n3 += 1

            P.gpsimd(lambda g: (P.wg(g, "SO0"), P.wg(g, "SO1")))
        _prog_cache[key] = nc
    return nc


def pf_weights(W1i, b1i, W2i, lngi, lnbi, b2i):
    W1p = np.zeros((D, 2 * HID), np.float32)
    W1p[:, 0:2730] = W1i[:, 0:2730]
    W1p[:, HID:HID + 2730] = W1i[:, 2730:5460]
    W2p = np.zeros((HID, D), np.float32)
    W2p[0:2730, :] = W2i
    b1a = np.zeros(HID, np.float32)
    b1a[0:2730] = b1i[0:2730]
    b1g = np.zeros(HID, np.float32)
    b1g[0:2730] = b1i[2730:5460]
    return dict(w1=np.ascontiguousarray(W1p), w2=np.ascontiguousarray(W2p),
                b1a=per_part(b1a), b1g=per_part(b1g),
                lng=per_part(lngi), lnb=per_part(lnbi), b2=per_part(b2i))


def run_pf(u_cores, wmap, last=False, x_cores=None):
    nc = build_pf(last=last)
    uext = ext_slices(u_cores, 1)
    in_maps = []
    for c in range(NCORES):
        m = dict(uin=uext[c], onc=ONES_COL, onr=ONES_ROW, eps=EPS5, **wmap)
        if last:
            m["xres"] = np.ascontiguousarray(np.asarray(x_cores[c], np.float32))
        in_maps.append(m)
    res = run_bass_kernel_spmd(nc, in_maps, core_ids=list(range(NCORES)))
    return [r["uout"] for r in res.results]


def kernel(**inputs):
    raise NotImplementedError("under construction")


HALO = 257
XW = HALO + T
NW = T // 256


def _rotary_base_tables():
    dim_head = 64
    window = 256
    half = np.arange(0, dim_head, 2, dtype=np.float64)
    inv_freq = 1.0 / 10000.0 ** (half / dim_head)
    pos = np.arange(-HALO, SEQ, dtype=np.float64)
    posc = np.clip(pos, 0, None)
    freqs_h = posc[None, :] * inv_freq[:, None]
    freqs = np.concatenate([freqs_h, freqs_h], axis=0)
    base_scale = (half + 0.4 * dim_head) / (1.4 * dim_head)
    power = (posc - SEQ // 2) / (window // 2)
    scale = base_scale[:, None] ** power[None, :]
    scale = np.concatenate([scale, scale], axis=0)
    cos = np.cos(freqs)
    sin = np.sin(freqs)
    sign = np.where(np.arange(64) % 64 < 32, -1.0, 1.0)[:, None]
    cq = 8.0 * cos * scale
    sq = 8.0 * sign * sin * scale
    ck = cos / scale
    sk = sign * sin / scale
    out = []
    for tname in (cq, sq, ck, sk):
        t128 = np.concatenate([tname, tname], axis=0)
        out.append(t128.astype(BF))
    return out


_TAB_CACHE = {}


def pa_tables(core):
    if "base" not in _TAB_CACHE:
        _TAB_CACHE["base"] = _rotary_base_tables()
    b, s = divmod(core, NB)
    lo = s * T
    tabs = [t[:, lo:lo + XW] for t in _TAB_CACHE["base"]]
    return np.ascontiguousarray(np.concatenate(tabs, axis=1))


def pa_masks(core):
    i = np.arange(128)[:, None]
    c = np.arange(384)[None, :]
    mR = np.zeros((128, 384), np.float32)
    mR[:, 0:128][c[:, 0:128] < i] = NEG
    d = c[:, 256:384] - 256
    mR[:, 256:384][d > i] = NEG
    b, s = divmod(core, NB)
    if s == 0:
        m0a = mR.copy()
        m0a[:, 0:256] = NEG
        m0b = mR.copy()
        m0b[:, 0:128] = NEG
    else:
        m0a = mR
        m0b = mR
    return np.ascontiguousarray(np.concatenate([mR, m0a, m0b], axis=1))


def pa_inds():
    ind_nq = np.zeros((128, 128), np.float32)
    ind_bc = np.zeros((16, 8 * 128), np.float32)
    for a in range(8):
        for p in range(128):
            h = 2 * a + p // 64
            ind_nq[p, a * 16 + h] = 1.0
            ind_bc[h, a * 128 + p] = 1.0
    perm = np.zeros((128, 128), np.float32)
    for p in range(128):
        q = (p + 32) % 64 + 64 * (p // 64) if p % 64 < 32 else (p - 32)
        perm[p, q] = 1.0
    return ind_nq, ind_bc, perm.astype(BF), np.eye(128).astype(BF)


def pa_scales(q_scale, k_scale):
    qs = np.asarray(q_scale, np.float32)
    ks = np.asarray(k_scale, np.float32)
    part = np.array([(p + 32) % 64 if p % 64 < 32 else (p % 64) - 32
                     for p in range(64)])
    def pp(v):
        return np.ascontiguousarray(np.tile(v, 2).reshape(128, 1))
    return pp(qs), pp(qs[part]), pp(ks), pp(ks[part])


def build_pa():
    if "pa" in _prog_cache:
        return _prog_cache["pa"]
    nc = bass.Bass("TRN2", target_bir_lowering=False, debug=False)
    stack = contextlib.ExitStack()
    with stack:
        uin = nc.dram_tensor("uin", [D, XW], F32, kind="ExternalInput").ap()
        wqkv = nc.dram_tensor("wqkv", [D, 3 * D], F32, kind="ExternalInput").ap()
        wo = nc.dram_tensor("wo", [D, D], F32, kind="ExternalInput").ap()
        tabs_d = nc.dram_tensor("tabs", [128, 4 * XW], BF16, kind="ExternalInput").ap()
        masks_d = nc.dram_tensor("masks", [128, 3 * 384], F32, kind="ExternalInput").ap()
        indnq_d = nc.dram_tensor("indnq", [128, 128], F32, kind="ExternalInput").ap()
        indbc_d = nc.dram_tensor("indbc", [16, 8 * 128], F32, kind="ExternalInput").ap()
        perm_d = nc.dram_tensor("perm", [128, 128], BF16, kind="ExternalInput").ap()
        eye_d = nc.dram_tensor("eye", [128, 128], BF16, kind="ExternalInput").ap()
        qs_d = nc.dram_tensor("qs", [128, 1], F32, kind="ExternalInput").ap()
        qsp_d = nc.dram_tensor("qsp", [128, 1], F32, kind="ExternalInput").ap()
        ks_d = nc.dram_tensor("ks", [128, 1], F32, kind="ExternalInput").ap()
        ksp_d = nc.dram_tensor("ksp", [128, 1], F32, kind="ExternalInput").ap()
        e12_d = nc.dram_tensor("e12", [1, 1], F32, kind="ExternalInput").ap()
        mh_d = nc.dram_tensor("mh", [16, 1], F32, kind="ExternalInput").ap()
        uout = nc.dram_tensor("uout", [D, T], F32, kind="ExternalOutput").ap()

        ub = sbt(nc, stack, "ub", [128, KT * XW], BF16)
        tabs = sbt(nc, stack, "tabs_s", [128, 4 * XW], BF16)
        tabf = sbt(nc, stack, "tabf", [128, 4 * XW], BF16)
        wq_s = sbt(nc, stack, "wq_s", [128, KT * 3 * D], BF16)
        wo_s = sbt(nc, stack, "wo_s", [128, KT * D], BF16)
        qh = sbt(nc, stack, "qh", [128, KT * 256], BF16)
        kspan = sbt(nc, stack, "kspan", [128, KT * 512], BF16)
        vb = sbt(nc, stack, "vb", [128, KT * 512], BF16)
        vtok = sbt(nc, stack, "vtok", [128, 4 * D], BF16)
        ao = sbt(nc, stack, "ao", [128, KT * 256], BF16)
        sqq = sbt(nc, stack, "sqq", [128, 512], F32)
        qnb = sbt(nc, stack, "qnb", [128, 512], BF16)
        t1b = sbt(nc, stack, "t1b", [128, 512], BF16)
        t2b = sbt(nc, stack, "t2b", [128, 512], BF16)
        nr = sbt(nc, stack, "nr", [16, 2 * 512], F32)
        atb = sbt(nc, stack, "atb", [128, 2 * 384], BF16)
        atT = sbt(nc, stack, "atT", [128, 2 * 384], BF16)
        ngm = sbt(nc, stack, "ngm", [128, 8], F32)
        masks = sbt(nc, stack, "masks_s", [128, 3 * 384], F32)
        indnq = sbt(nc, stack, "indnq_s", [128, 128], F32)
        indbc = sbt(nc, stack, "indbc_s", [16, 8 * 128], F32)
        perm = sbt(nc, stack, "perm_s", [128, 128], BF16)
        eye = sbt(nc, stack, "eye_s", [128, 128], BF16)
        scl = sbt(nc, stack, "scl", [128, 8], F32)
        sclb = sbt(nc, stack, "sclb", [128, 4], BF16)
        e12 = sbt(nc, stack, "e12_s", [1, 1], F32)
        mh16 = sbt(nc, stack, "mh16", [16, 1], F32)
        urs = [sbt(nc, stack, f"urs{i}", [128, 256], F32) for i in range(2)]
        uo = [sbt(nc, stack, f"uo{i}", [128, 256], F32) for i in range(2)]

        p_qkv = [pst(nc, stack, f"p_qkv{i}", [128, 512]) for i in range(2)]
        p_n = pst(nc, stack, "p_n", [128, 512])
        p_rot = pst(nc, stack, "p_rot", [128, 512])
        p_rb = pst(nc, stack, "p_rb", [128, 512])
        p_bt = pst(nc, stack, "p_bt", [128, 512], BF16)
        p_av = pst(nc, stack, "p_av", [128, 128])
        p_wo = pst(nc, stack, "p_wo", [128, 256])

        with nc.Block() as block:
            P = Em(nc, block, stack)

            def loads(q):
                P.inc(q.dma_start(out=tabs[:, :], in_=tabs_d[:, :]), "Lc", 16)
                P.inc(q.dma_start(out=masks[:, :], in_=masks_d[:, :]), "Lc", 16)
                P.inc(q.dma_start(out=indnq[:, :], in_=indnq_d[:, :]), "Lc", 16)
                P.inc(q.dma_start(out=indbc[:, :], in_=indbc_d[:, :]), "Lc", 16)
                P.inc(q.dma_start(out=perm[:, :], in_=perm_d[:, :]), "Lc", 16)
                P.inc(q.dma_start(out=eye[:, :], in_=eye_d[:, :]), "Lc", 16)
                P.inc(q.dma_start(out=scl[:, 0:1], in_=qs_d[:, :]), "Lc", 16)
                P.inc(q.dma_start(out=scl[:, 1:2], in_=qsp_d[:, :]), "Lc", 16)
                P.inc(q.dma_start(out=scl[:, 2:3], in_=ks_d[:, :]), "Lc", 16)
                P.inc(q.dma_start(out=scl[:, 3:4], in_=ksp_d[:, :]), "Lc", 16)
                P.inc(q.dma_start(out=e12[:, :], in_=e12_d[:, :]), "Lc", 16)
                P.inc(q.dma_start(out=mh16[:, :], in_=mh_d[:, :]), "Lc", 16)
            P.sync(loads)
            n_const = P.cnt["Lc"]

            def casts(g):
                P.inc(g.dma_start(out=ub.rearrange("p (k t) -> p k t", k=KT),
                                  in_=kpt(uin)), "Lu", 16)
                P.inc(g.dma_start(out=wq_s.rearrange("p (k m) -> p k m", k=KT),
                                  in_=wqkv.rearrange("(k p) m -> p k m", p=128)), "Lw", 16)
                P.inc(g.dma_start(out=wo_s.rearrange("p (k m) -> p k m", k=KT),
                                  in_=wo.rearrange("(k p) m -> p k m", p=128)), "Lw", 16)
            P.gpsimd(casts)

            def fold(v):
                P.wg(v, "Lc", n_const)
                for i in range(4):
                    ins = v.tensor_scalar_mul(tabf[:, i * XW:(i + 1) * XW],
                                              tabs[:, i * XW:(i + 1) * XW],
                                              scl[:, i:i + 1])
                    if i == 3:
                        P.inc(ins, "TF")
            P.vector(fold)

            for w in range(NW):
                Wn = 512 if w == 0 else 256
                eb = 1 if w == 0 else HALO + w * 256
                qc0 = 256 if w == 0 else 0
                span0 = 1 + w * 256

                if w >= 1:
                    def roll(v, w=w):
                        P.wg(v, "SCM", P.cnt.get("SCM", 0))
                        P.wg(v, "AVM", P.cnt.get("AVM", 0))
                        for a in range(KT):
                            v.tensor_copy(kspan[:, a * 512:a * 512 + 256],
                                          kspan[:, a * 512 + 256:a * 512 + 512])
                        v.tensor_copy(vtok[:, 0:2 * D], vtok[:, 2 * D:4 * D])
                        P.inc(v.engine_nop(), "RL")
                    P.vector(roll)

                for e in range(24):
                    jb = e % 2
                    kind = "q" if e < 8 else ("k" if e < 16 else "v")
                    a = e % 8

                    def mm_qkv(t, e=e, jb=jb, w=w, Wn=Wn, eb=eb):
                        if w == 0 and e == 0:
                            P.wg(t, "Lu", 16)
                            P.wg(t, "Lw", 32)
                        if w >= 1 and e == 0:
                            P.wg(t, "RL", P.cnt["RL"])
                        n = w * 24 + e
                        if n >= 2:
                            P.wg(t, "QE", n - 1)
                        for k in range(KT):
                            off = eb if k < 4 else eb - 1
                            rhs = ub[:, k * XW + off: k * XW + off + Wn]
                            ins = t.matmul(p_qkv[jb][:, 0:Wn],
                                           wq_s[:, k * 3 * D + e * 128:(k * 3 * D) + (e + 1) * 128],
                                           rhs, start=(k == 0), stop=(k == KT - 1))
                            if k == KT - 1:
                                P.inc(ins, "QM")
                    P.tensor(mm_qkv)

                    if kind in ("q", "k"):
                        def sq_ev(s_, e=e, jb=jb, a=a, kind=kind, w=w, Wn=Wn, qc0=qc0):
                            P.wg(s_, "QM", P.cnt["QM"])
                            if kind == "q":
                                ins = s_.activation(sqq[:, 0:256], p_qkv[jb][:, qc0:qc0 + 256], AF.Square)
                            else:
                                ins = s_.activation(sqq[:, 0:Wn], p_qkv[jb][:, 0:Wn], AF.Square)
                            P.inc(ins, "SQE")
                        P.scalar(sq_ev)

                        def qk_cp(v, e=e, jb=jb, a=a, kind=kind, w=w, Wn=Wn, qc0=qc0):
                            P.wg(v, "QM", P.cnt["QM"])
                            if w >= 1 and a == 0:
                                P.wg(v, "SCM", P.cnt.get("SCM", 0))
                            if kind == "k" and w >= 1 and a == 0:
                                P.wg(v, "RL", P.cnt["RL"])
                            if kind == "q":
                                ins = v.tensor_copy(qh[:, a * 256:(a + 1) * 256], p_qkv[jb][:, qc0:qc0 + 256])
                            else:
                                dst = kspan[:, a * 512 + 512 - Wn: a * 512 + 512]
                                ins = v.tensor_copy(dst, p_qkv[jb][:, 0:Wn])
                            P.inc(ins, "QE")
                        P.vector(qk_cp)

                        def nsum(t, e=e, a=a, kind=kind, w=w, Wn=Wn):
                            P.wg(t, "SQE", P.cnt["SQE"])
                            if a == 0 and w * 24 + e >= 2:
                                P.wg(t, "NRD", P.cnt.get("NRD", 0))
                            r0 = 0 if kind == "q" else 64
                            wd = 256 if kind == "q" else Wn
                            ins = t.matmul(p_n[r0:r0 + 16, 0:wd],
                                           indnq[:, a * 16:(a + 1) * 16],
                                           sqq[:, 0:wd], start=(a == 0), stop=(a == 7))
                            if a == 7:
                                P.inc(ins, "NM")
                        P.tensor(nsum)
                    else:
                        def v_ev(v, jb=jb, a=a, Wn=Wn, w=w):
                            P.wg(v, "QM", P.cnt["QM"])
                            if w >= 1 and a == 0:
                                P.wg(v, "VT", P.cnt.get("VT", 0))
                            ins = v.tensor_copy(vb[:, a * 512 + 512 - Wn: a * 512 + 512], p_qkv[jb][:, 0:Wn])
                            P.inc(ins, "QE")
                        P.vector(v_ev)

                def nrow_v(v, w=w, Wn=Wn):
                    P.wg(v, "NM", P.cnt["NM"])
                    if w >= 1:
                        P.wg(v, "BC", P.cnt.get("BC", 0))
                    v.tensor_scalar_max(nr[:, 0:256], p_n[0:16, 0:256], 1e-24)
                    P.inc(v.tensor_scalar_max(nr[:, 512:512 + Wn], p_n[64:80, 0:Wn], 1e-24), "NV")
                P.vector(nrow_v)

                def nrow_s(s_, w=w, Wn=Wn):
                    P.wg(s_, "NV", P.cnt["NV"])
                    P.inc(s_.activation(nr[:, 0:256], nr[:, 0:256], AF.Ln), "NRD")
                    s_.activation(nr[:, 512:512 + Wn], nr[:, 512:512 + Wn], AF.Ln)
                    P.wg(s_, "NRD")
                    s_.activation(nr[:, 0:256], nr[:, 0:256], AF.Exp, scale=mh16[:, 0:1])
                    P.inc(s_.activation(nr[:, 512:512 + Wn], nr[:, 512:512 + Wn], AF.Exp,
                                        scale=mh16[:, 0:1]), "NRD")
                P.scalar(nrow_s)

                for tensor_kind in ("q", "k"):
                    for a in range(KT):
                        Wb = 256 if tensor_kind == "q" else Wn
                        if tensor_kind == "q":
                            tb0 = HALO + w * 256
                        else:
                            tb0 = span0 + 512 - Wn - (0 if w == 0 else 256)
                            tb0 = span0 if w == 0 else HALO + w * 256
                        ci = 0 if tensor_kind == "q" else 2
                        si = 1 if tensor_kind == "q" else 3
                        rrow = 0 if tensor_kind == "q" else 512
                        src = qh[:, a * 256:(a + 1) * 256] if tensor_kind == "q" else \
                            kspan[:, a * 512 + 512 - Wn: a * 512 + 512]

                        def bc_mm(t, a=a, rrow=rrow, Wb=Wb):
                            P.wg(t, "NRD", P.cnt["NRD"])
                            P.wg(t, "QHB", P.cnt.get("QHB", 0))
                            P.inc(t.matmul(p_rb[:, 0:Wb], indbc[:, a * 128:(a + 1) * 128],
                                           nr[:, rrow:rrow + Wb], start=True, stop=True), "BC")
                        P.tensor(bc_mm)

                        def qn_mul(v, src=src, Wb=Wb):
                            P.wg(v, "BC", P.cnt["BC"])
                            P.wg(v, "QE", P.cnt["QE"])
                            P.inc(v.tensor_mul(qnb[:, 0:Wb], src, p_rb[:, 0:Wb]), "QN")
                        P.vector(qn_mul)

                        def rot_mm(t, Wb=Wb):
                            P.wg(t, "QN", P.cnt["QN"])
                            P.wg(t, "T2D", P.cnt.get("T2D", 0))
                            P.inc(t.matmul(p_rot[:, 0:Wb], perm[:, :], qnb[:, 0:Wb],
                                           start=True, stop=True), "RT")
                        P.tensor(rot_mm)

                        def hat_build(v, a=a, Wb=Wb, ci=ci, si=si, tb0=tb0, src=src,
                                      tensor_kind=tensor_kind, w=w):
                            if w == 0 and a == 0 and tensor_kind == "q":
                                P.wg(v, "TF", P.cnt["TF"])
                            P.wg(v, "RT", P.cnt["RT"])
                            v.tensor_mul(t1b[:, 0:Wb], qnb[:, 0:Wb],
                                         tabf[:, ci * XW + tb0: ci * XW + tb0 + Wb])
                            P.inc(v.tensor_mul(t2b[:, 0:Wb], p_rot[:, 0:Wb],
                                               tabf[:, si * XW + tb0: si * XW + tb0 + Wb]), "T2D")
                            P.wg(v, "T2D")
                            P.inc(v.tensor_add(src, t1b[:, 0:Wb], t2b[:, 0:Wb]), "QHB")
                        P.vector(hat_build)

                nnew = 4 if w == 0 else 2
                for tb in range(nnew):
                    blk = tb if w == 0 else 2 + tb
                    for a in range(KT):
                        def vtr(t, tb=tb, a=a, w=w, Wn=Wn):
                            P.wg(t, "QE", P.cnt["QE"])
                            if w >= 1 and tb == 0 and a == 0:
                                P.wg(t, "ATE", P.cnt.get("ATE", 0))
                            if w > 0 or tb > 0 or a > 0:
                                P.wg(t, "VTE", P.cnt["VTE"])
                            src_col = a * 512 + 512 - Wn + tb * 128
                            P.inc(t.transpose(p_bt[:, 0:128],
                                              vb[:, src_col:src_col + 128], eye[:, :]), "VT")
                        P.tensor(vtr)

                        def vte(v, blk=blk, a=a):
                            P.wg(v, "VT", P.cnt["VT"])
                            P.inc(v.tensor_copy(vtok[:, blk * D + a * 128: blk * D + (a + 1) * 128],
                                                p_bt[:, 0:128]), "VTE")
                        P.vector(vte)

                for h in range(16):
                    for qt in range(2):
                        nat = w * 32 + h * 2 + qt
                        jb = nat % 2
                        a = h // 2
                        r0 = (h % 2) * 64

                        def sc_mm(t, a=a, r0=r0, qt=qt, jb=jb, nat=nat):
                            P.wg(t, "QHB", P.cnt["QHB"])
                            if nat >= 2:
                                P.wg(t, "EXD", nat - 1)
                            lhs = qh[r0:r0 + 64, a * 256 + qt * 128: a * 256 + (qt + 1) * 128]
                            rhs = kspan[r0:r0 + 64, a * 512 + qt * 128: a * 512 + qt * 128 + 384]
                            P.inc(t.matmul(p_qkv[jb][:, 0:384], lhs, rhs,
                                           start=True, stop=True), "SCM")
                        P.tensor(sc_mm)

                        pu = nat % 2
                        ab = atb[:, pu * 384:(pu + 1) * 384]
                        aT = atT[:, pu * 384:(pu + 1) * 384]
                        ng = ngm[:, pu * 4:pu * 4 + 3]

                        def sc_mask(v, jb=jb, w=w, qt=qt, nat=nat, ng=ng):
                            P.wg(v, "SCM", P.cnt["SCM"])
                            if nat >= 2:
                                P.wg(v, "EXD", nat - 1)
                            moff = (1 + qt) * 384 if w == 0 else 0
                            P.inc(v.tensor_add(p_qkv[jb][:, 0:384], p_qkv[jb][:, 0:384],
                                               masks[:, moff:moff + 384]), "MKD")
                            P.wg(v, "MKD")
                            P.inc(v.reduce_max(ng[:, 0:1], p_qkv[jb][:, 0:384],
                                               axis=AX.X, negate=True), "NGM")
                        P.vector(sc_mask)

                        def sc_exp(s_, jb=jb, nat=nat, ab=ab, ng=ng):
                            P.wg(s_, "NGM", P.cnt["NGM"])
                            if nat >= 2:
                                P.wg(s_, "ATT", nat - 1)
                                P.wg(s_, "RSD", nat - 1)
                            P.inc(s_.activation(ab, p_qkv[jb][:, 0:384], AF.Exp,
                                                bias=ng[:, 0:1], accum_out=ng[:, 1:2]), "EXD")
                        P.scalar(sc_exp)

                        def sc_nrm(v, ab=ab, ng=ng):
                            P.wg(v, "EXD", P.cnt["EXD"])
                            P.inc(v.reciprocal(ng[:, 2:3], ng[:, 1:2]), "RSD")
                            P.wg(v, "RSD")
                            P.inc(v.tensor_scalar_mul(ab, ab, ng[:, 2:3]), "ATN")
                        P.vector(sc_nrm)

                        def at_tr(t, nat=nat, h=h, qt=qt, ab=ab):
                            P.wg(t, "ATN", P.cnt["ATN"])
                            if h == 0 and qt == 0:
                                P.wg(t, "VTE", P.cnt["VTE"])
                            if nat >= 1:
                                P.wg(t, "ATE", P.cnt["ATE"])
                            for kb in range(3):
                                ins = t.transpose(p_bt[:, 128 + kb * 128: 256 + kb * 128],
                                                  ab[:, kb * 128:(kb + 1) * 128], eye[:, :])
                                if kb == 2:
                                    P.inc(ins, "ATT")
                        P.tensor(at_tr)

                        def at_ev(v, nat=nat, aT=aT):
                            P.wg(v, "ATT", P.cnt["ATT"])
                            if nat >= 2:
                                P.wg(v, "AVM", nat - 1)
                            P.inc(v.tensor_copy(aT, p_bt[:, 128:512]), "ATE")
                        P.vector(at_ev)

                        def av_mm(t, h=h, qt=qt, r0=r0, nat=nat, aT=aT):
                            P.wg(t, "ATE", P.cnt["ATE"])
                            if nat >= 1:
                                P.wg(t, "AVE", P.cnt["AVE"])
                            for kb in range(3):
                                lhs = vtok[:, (qt + kb) * D + h * 64:(qt + kb) * D + h * 64 + 64]
                                ins = t.matmul(p_av[r0:r0 + 64, 0:128], lhs,
                                               aT[:, kb * 128:(kb + 1) * 128],
                                               start=(kb == 0), stop=(kb == 2))
                                if kb == 2:
                                    P.inc(ins, "AVM")
                        P.tensor(av_mm)

                        def av_ev(v, h=h, qt=qt, r0=r0, a=a, w=w):
                            P.wg(v, "AVM", P.cnt["AVM"])
                            if w >= 1 and h == 0 and qt == 0:
                                P.wg(v, "WOM", P.cnt.get("WOM", 0))
                            P.inc(v.tensor_copy(ao[r0:r0 + 64, a * 256 + qt * 128: a * 256 + (qt + 1) * 128],
                                                p_av[r0:r0 + 64, 0:128]), "AVE")
                        P.vector(av_ev)

                for mo in range(KT):
                    nwo = w * KT + mo
                    j = nwo % 2

                    def ures(q, mo=mo, j=j, w=w, nwo=nwo):
                        if nwo >= 2:
                            P.wg(q, "EVW", nwo - 1)
                        P.inc(q.dma_start(
                            out=urs[j][:, :],
                            in_=uin[mo * 128:(mo + 1) * 128, HALO + w * 256: HALO + (w + 1) * 256]),
                            f"Lur{j}", 16)
                    P.sync(ures)

                    def wo_mm(t, mo=mo, w=w, nwo=nwo):
                        P.wg(t, "AVE", P.cnt["AVE"])
                        if nwo >= 1:
                            P.wg(t, "EVW", nwo)
                        for a in range(KT):
                            ins = t.matmul(p_wo[:, :], wo_s[:, a * D + mo * 128: a * D + (mo + 1) * 128],
                                           ao[:, a * 256:(a + 1) * 256],
                                           start=(a == 0), stop=(a == KT - 1))
                            if a == KT - 1:
                                P.inc(ins, "WOM")
                    P.tensor(wo_mm)

                    def evac(v, j=j, nwo=nwo):
                        P.wg(v, "WOM", P.cnt["WOM"])
                        P.wg(v, f"Lur{j}", P.cnt[f"Lur{j}"])
                        if nwo >= 2:
                            P.wg(v, f"SO{j}", 16 * (nwo // 2))
                        P.inc(v.tensor_add(uo[j][:, :], p_wo[:, :], urs[j][:, :]), "EVW")
                    P.vector(evac)

                    def store(g, mo=mo, j=j, w=w):
                        P.wg(g, "EVW", P.cnt["EVW"])
                        P.inc(g.dma_start(
                            out=uout[mo * 128:(mo + 1) * 128, w * 256:(w + 1) * 256],
                            in_=uo[j][:, :]), f"SO{j}", 16)
                    P.gpsimd(store)

            P.gpsimd(lambda g: (P.wg(g, "SO0"), P.wg(g, "SO1")))
        _prog_cache["pa"] = nc
    return nc


def pa_weights(Wqkvi, Woi):
    return dict(wqkv=np.ascontiguousarray(np.asarray(Wqkvi, np.float32)),
                wo=np.ascontiguousarray(np.asarray(Woi, np.float32)))


E12 = np.full((1, 1), 1e-12, np.float32)
MHALF16 = np.full((16, 1), -0.5, np.float32)


def run_pa(u_cores, wmap, qsi, ksi):
    nc = build_pa()
    uext = ext_slices(u_cores, HALO)
    indnq, indbc, perm, eye = pa_inds()
    qs, qsp, ks, ksp = pa_scales(qsi, ksi)
    in_maps = []
    for c in range(NCORES):
        in_maps.append(dict(uin=uext[c], tabs=pa_tables(c), masks=pa_masks(c),
                            indnq=indnq, indbc=indbc, perm=perm, eye=eye,
                            qs=qs, qsp=qsp, ks=ks, ksp=ksp, e12=E12,
                            mh=MHALF16, **wmap))
    res = run_bass_kernel_spmd(nc, in_maps, core_ids=list(range(NCORES)))
    return [r["uout"] for r in res.results]


NE = 8
NTB = T // 128


def build_pmoe(upto=9):
    key = f"pmoe{upto}"
    if key in _prog_cache:
        return _prog_cache[key]
    nc = bass.Bass("TRN2", target_bir_lowering=False, debug=False)
    stack = contextlib.ExitStack()
    with stack:
        uin = nc.dram_tensor("uin", [D, T], F32, kind="ExternalInput").ap()
        xin = nc.dram_tensor("xin", [D, T], F32, kind="ExternalInput").ap()
        gwT = nc.dram_tensor("gwT", [128, KT * NE], F32, kind="ExternalInput").ap()
        upw = nc.dram_tensor("upw", [NE * D, D], F32, kind="ExternalInput").ap()
        dnw = nc.dram_tensor("dnw", [NE * D, D], F32, kind="ExternalInput").ap()
        upb = nc.dram_tensor("upb", [128, NE * KT], F32, kind="ExternalInput").ap()
        dnb = nc.dram_tensor("dnb", [NE, D], F32, kind="ExternalInput").ap()
        shup = nc.dram_tensor("shup", [D, D], F32, kind="ExternalInput").ap()
        shdn = nc.dram_tensor("shdn", [D, D], F32, kind="ExternalInput").ap()
        shupb = nc.dram_tensor("shupb", [128, KT], F32, kind="ExternalInput").ap()
        shdnb = nc.dram_tensor("shdnb", [128, KT], F32, kind="ExternalInput").ap()
        onrb_d = nc.dram_tensor("onrb", [1, 128], BF16, kind="ExternalInput").ap()
        selb_d = nc.dram_tensor("selb", [NE, NE * 128], BF16, kind="ExternalInput").ap()
        eye_d = nc.dram_tensor("eye", [128, 128], BF16, kind="ExternalInput").ap()
        uout = nc.dram_tensor("uout", [D, T], F32, kind="ExternalOutput").ap()

        ub = sbt(nc, stack, "ub", [128, KT * T], BF16)
        he = sbt(nc, stack, "he", [128, KT * T], BF16)
        acc = sbt(nc, stack, "acc", [128, KT * T], F32)
        ws = [sbt(nc, stack, f"ws{i}", [128, KT * D], BF16) for i in range(3)]
        ugf = [sbt(nc, stack, f"ugf{i}", [128, KT * 128], F32) for i in range(2)]
        grow = sbt(nc, stack, "grow", [128, 32], F32)
        cbt = sbt(nc, stack, "cbt", [128, NE], BF16)
        combf = sbt(nc, stack, "combf", [NE, T], BF16)
        selb = sbt(nc, stack, "selb_s", [NE, NE * 128], BF16)
        dnbs = sbt(nc, stack, "dnbs", [NE, D], BF16)
        gws = sbt(nc, stack, "gws", [128, KT * NE], F32)
        prm = sbt(nc, stack, "prm", [128, NE * KT + 2 * KT], F32)
        onrb = sbt(nc, stack, "onrb_s", [1, 128], BF16)
        eye = sbt(nc, stack, "eye_s", [128, 128], BF16)
        xrs = [sbt(nc, stack, f"xrs{i}", [128, TCW], F32) for i in range(2)]
        uo = [sbt(nc, stack, f"uo{i}", [128, TCW], F32) for i in range(2)]

        p_g = pst(nc, stack, "p_g", [128, NE])
        p_tr = pst(nc, stack, "p_tr", [128, 128], BF16)
        p_u = [pst(nc, stack, f"p_u{i}", [128, TCW]) for i in range(2)]
        p_d = [pst(nc, stack, f"p_d{i}", [128, TCW]) for i in range(2)]
        p_cb = pst(nc, stack, "p_cb", [128, TCW])

        with nc.Block() as block:
            P = Em(nc, block, stack)

            def consts(q):
                P.inc(q.dma_start(out=gws[:, :], in_=gwT[:, :]), "Lc", 16)
                P.inc(q.dma_start(out=prm[:, 0:NE * KT], in_=upb[:, :]), "Lc", 16)
                P.inc(q.dma_start(out=prm[:, NE * KT:NE * KT + KT], in_=shupb[:, :]), "Lc", 16)
                P.inc(q.dma_start(out=prm[:, NE * KT + KT:NE * KT + 2 * KT], in_=shdnb[:, :]), "Lc", 16)
                P.inc(q.dma_start(out=onrb[:, :], in_=onrb_d[:, :]), "Lc", 16)
                P.inc(q.dma_start(out=eye[:, :], in_=eye_d[:, :]), "Lc", 16)
                P.inc(q.dma_start(out=selb[:, :], in_=selb_d[:, :]), "Lc", 16)
            P.sync(consts)
            n_const = P.cnt["Lc"]

            def casts(g):
                P.inc(g.dma_start(out=ub.rearrange("p (k t) -> p k t", k=KT),
                                  in_=kpt(uin)), "Lub", 16)
                P.inc(g.dma_start(out=dnbs[:, :], in_=dnb[:, :]), "Ldb", 16)
            P.gpsimd(casts)

            for tb in range([] if upto == 10 else range(NTB)) if False else (range(0) if upto == 10 else range(NTB)):
                i = tb % 2

                def gload(q, tb=tb, i=i):
                    if tb >= 2:
                        P.wg(q, "GM", tb - 1)
                    P.inc(q.dma_start(
                        out=ugf[i].rearrange("p (k t) -> p k t", k=KT),
                        in_=kpt(uin)[:, :, tb * 128:(tb + 1) * 128]), f"Lg{i}", 16)
                P.sync(gload)

                def g_mm(t, tb=tb, i=i):
                    P.wg(t, f"Lg{i}", 16 * (tb // 2 + 1))
                    if tb == 0:
                        P.wg(t, "Lc", n_const)
                    if tb >= 1:
                        P.wg(t, "GS", tb)
                    for k in range(KT):
                        ins = t.matmul(p_g[:, :], ugf[i][:, k * 128:(k + 1) * 128],
                                       gws[:, k * NE:(k + 1) * NE],
                                       start=(k == 0), stop=(k == KT - 1))
                        if k == KT - 1:
                            P.inc(ins, "GM")
                P.tensor(g_mm)

                def g_v1(v, tb=tb):
                    P.wg(v, "GM", P.cnt["GM"])
                    P.inc(v.reduce_max(grow[:, 0:1], p_g[:, :], axis=AX.X, negate=True), "GV")
                P.vector(g_v1)

                def g_s1(s_, tb=tb):
                    P.wg(s_, "GV", P.cnt["GV"])
                    P.inc(s_.activation(grow[:, 8:16], p_g[:, :], AF.Exp,
                                        bias=grow[:, 0:1], accum_out=grow[:, 2:3]), "GS")
                P.scalar(g_s1)

                def g_v2(v, tb=tb):
                    P.wg(v, "GS", P.cnt["GS"])
                    if tb >= 1:
                        P.wg(v, "CBT", tb)
                    st = grow[:, 8:16]
                    P.inc(v.reciprocal(grow[:, 3:4], grow[:, 2:3]), "GV")
                    P.wg(v, "GV")
                    P.inc(v.tensor_scalar_mul(st, st, grow[:, 3:4]), "GV")
                    P.wg(v, "GV")
                    P.inc(v.reduce_max(grow[:, 0:1], st, axis=AX.X), "GV")
                    P.wg(v, "GV")
                    P.inc(v.tensor_scalar(grow[:, 16:24], st, grow[:, 0:1], None,
                                          ALU.is_equal), "GV")
                    P.wg(v, "GV")
                    P.inc(v.scalar_tensor_tensor(grow[:, 16:24], grow[:, 16:24], NEG,
                                                 st, ALU.mult, ALU.add), "GV")
                    P.wg(v, "GV")
                    P.inc(v.reduce_max(grow[:, 1:2], grow[:, 16:24], axis=AX.X), "GV")
                    P.wg(v, "GV")
                    P.inc(v.tensor_scalar(grow[:, 24:32], st, grow[:, 1:2], None,
                                          ALU.is_ge), "GV")
                    P.wg(v, "GV")
                    P.inc(v.tensor_add(grow[:, 4:5], grow[:, 0:1], grow[:, 1:2]), "GV")
                    P.wg(v, "GV")
                    P.inc(v.tensor_scalar_add(grow[:, 4:5], grow[:, 4:5], 1e-20), "GV")
                    P.wg(v, "GV")
                    P.inc(v.reciprocal(grow[:, 5:6], grow[:, 4:5]), "GV")
                    P.wg(v, "GV")
                    P.inc(v.tensor_mul(st, st, grow[:, 24:32]), "GV")
                    P.wg(v, "GV")
                    P.inc(v.tensor_scalar_mul(cbt[:, :], st, grow[:, 5:6]), "GV")
                P.vector(g_v2)

                def g_tr(t, tb=tb):
                    P.wg(t, "GV", P.cnt["GV"])
                    if tb >= 1:
                        P.wg(t, "CBE", tb)
                    P.inc(t.transpose(p_tr[0:NE, 0:128], cbt[:, :], eye[:, :]), "CBT")
                P.tensor(g_tr)

                def g_ev(s_, tb=tb):
                    P.wg(s_, "CBT", P.cnt["CBT"])
                    P.inc(s_.copy(combf[:, tb * 128:(tb + 1) * 128], p_tr[0:NE, 0:128]), "CBE")
                P.scalar(g_ev)

            if upto <= 1:
                def dbg(v):
                    P.wg(v, "CBE", NTB)
                    P.inc(v.tensor_copy(acc[0:NE, 0:T], combf[:, :]), "DBG")
                P.vector(dbg)
                def dbgst(g):
                    for s_ in list(P.cnt):
                        if s_.startswith("L"):
                            P.wg(g, s_)
                    P.wg(g, "DBG", 1)
                    P.inc(g.dma_start(out=uout[0:NE, :], in_=acc[0:NE, 0:T]), "SO", 16)
                    P.wg(g, "SO")
                P.gpsimd(dbgst)
                _prog_cache[key] = nc
                return nc

            def wload(g, src_ap, slot, sem="Lw"):
                P.inc(g.dma_start(out=ws[slot].rearrange("p (k m) -> p k m", k=KT),
                                  in_=src_ap), "Lw", 16)

            nW = 0
            NE_RUN = 1 if upto in (10, 11, 12, 13) else NE
            for e in range(NE_RUN):
                su = (2 * e) % 3
                sd = (2 * e + 1) % 3

                def wl(g, e=e, su=su, sd=sd):
                    if e >= 2:
                        P.wg(g, "DNM", (e - 1) * 32)
                    P.inc(g.dma_start(out=ws[su].rearrange("p (k m) -> p k m", k=KT),
                                      in_=upw.rearrange("(e k p) m -> p e k m", p=128, e=NE)[:, e]),
                          f"Lw{su}", 16)
                    if e >= 1:
                        P.wg(g, "UPM", e * 32)
                    P.inc(g.dma_start(out=ws[sd].rearrange("p (k m) -> p k m", k=KT),
                                      in_=dnw.rearrange("(e k p) m -> p e k m", p=128, e=NE)[:, e]),
                          f"Lw{sd}", 16)
                P.gpsimd(wl)
                nW += 2

                for mh in range(KT):
                    for tc in range(TC):
                        nuu = (e * KT + mh) * TC + tc
                        j = nuu % 2

                        def up_mm(t, e=e, mh=mh, tc=tc, j=j, nuu=nuu, su=su,
                                  lwv=None):
                            if e == 0 and mh == 0 and tc == 0:
                                P.wg(t, "Lub", 16)
                            P.wg(t, f"Lw{su}", P.cnt[f"Lw{su}"])
                            if nuu >= 2:
                                P.wg(t, "HSI", nuu - 1)
                            for k in range(KT):
                                ins = t.matmul(p_u[j][:, :],
                                               ws[su][:, k * D + mh * 128:k * D + (mh + 1) * 128],
                                               ub[:, k * T + tc * TCW:k * T + (tc + 1) * TCW],
                                               start=(k == 0), stop=(k == KT - 1))
                                if k == KT - 1:
                                    P.inc(ins, "UPM")
                        P.tensor(up_mm)

                        def up_silu(s_, e=e, mh=mh, j=j, nuu=nuu):
                            P.wg(s_, "UPM", P.cnt["UPM"])
                            P.inc(s_.activation(he[:, mh * T + (nuu % TC) * TCW: mh * T + (nuu % TC + 1) * TCW]
                                                if False else he[:, mh * T + (nuu % TC) * TCW: mh * T + ((nuu % TC) + 1) * TCW],
                                                p_u[j][:, :], AF.Silu,
                                                bias=prm[:, e * KT + mh: e * KT + mh + 1]), "HSI")
                        P.scalar(up_silu)

                if upto in (10, 11) and e == 0:
                    break
                for mh in range(KT):
                    for tc in range(TC):
                        nsc = (e * KT + mh) * TC + tc

                        def cb_mm(t, e=e, tc=tc, nsc=nsc):
                            P.wg(t, "HSI", P.cnt["HSI"])
                            if nsc >= 1:
                                P.wg(t, "HSC", P.cnt["HSC"])
                            P.inc(t.matmul(p_cb[:, :], selb[:, e * 128:(e + 1) * 128],
                                           combf[:, tc * TCW:(tc + 1) * TCW],
                                           start=True, stop=True), "CBM")
                        P.tensor(cb_mm)

                        def h_scale(v, mh=mh, tc=tc):
                            P.wg(v, "CBM", P.cnt["CBM"])
                            sl = he[:, mh * T + tc * TCW: mh * T + (tc + 1) * TCW]
                            P.inc(v.tensor_mul(sl, sl, p_cb[:, :]), "HSC")
                        P.vector(h_scale)

                if upto == 12 and e == 0:
                    break
                for mo in range(KT):
                    for tc in range(TC):
                        ndd = (e * KT + mo) * TC + tc
                        j = ndd % 2

                        def dn_mm(t, e=e, mo=mo, tc=tc, j=j, ndd=ndd, sd=sd):
                            P.wg(t, "HSC", P.cnt["HSC"])
                            P.wg(t, f"Lw{sd}", P.cnt[f"Lw{sd}"])
                            if e == 0 and mo == 0 and tc == 0:
                                P.wg(t, "Ldb", 16)
                            if ndd >= 2:
                                P.wg(t, "ACC", ndd - 1)
                            for k in range(KT):
                                last = (k == KT - 1) and (e != 0)
                                ins = t.matmul(p_d[j][:, :],
                                               ws[sd][:, k * D + mo * 128:k * D + (mo + 1) * 128],
                                               he[:, k * T + tc * TCW:k * T + (tc + 1) * TCW],
                                               start=(k == 0), stop=last)
                            if e == 0:
                                ins = t.matmul(p_d[j][:, :], dnbs[:, mo * 128:(mo + 1) * 128],
                                               combf[:, tc * TCW:(tc + 1) * TCW],
                                               start=False, stop=True)
                            P.inc(ins, "DNM")
                        P.tensor(dn_mm)

                        def acc_add(v, e=e, mo=mo, tc=tc, j=j):
                            P.wg(v, "DNM", P.cnt["DNM"])
                            sl = acc[:, mo * T + tc * TCW: mo * T + (tc + 1) * TCW]
                            if e == 0:
                                P.inc(v.tensor_copy(sl, p_d[j][:, :]), "ACC")
                            else:
                                P.inc(v.tensor_add(sl, sl, p_d[j][:, :]), "ACC")
                        P.vector(acc_add)

            if upto in (10, 11):
                def dbgst(g):
                    for s_ in list(P.cnt):
                        if s_.startswith("L"):
                            P.wg(g, s_)
                    P.wg(g, "HSI", P.cnt["HSI"])
                    P.inc(g.dma_start(out=kpt(uout), in_=he.rearrange("p (k t) -> p k t", k=KT)), "SO", 16)
                    P.wg(g, "SO")
                P.gpsimd(dbgst)
                _prog_cache[key] = nc
                return nc
            if upto == 12:
                def dbgst(g):
                    for s_ in list(P.cnt):
                        if s_.startswith("L"):
                            P.wg(g, s_)
                    P.wg(g, "HSC", P.cnt["HSC"])
                    P.inc(g.dma_start(out=kpt(uout), in_=he.rearrange("p (k t) -> p k t", k=KT)), "SO", 16)
                    P.wg(g, "SO")
                P.gpsimd(dbgst)
                _prog_cache[key] = nc
                return nc
            if upto == 13:
                def dbgst(g):
                    for s_ in list(P.cnt):
                        if s_.startswith("L"):
                            P.wg(g, s_)
                    P.wg(g, "ACC", P.cnt["ACC"])
                    P.inc(g.dma_start(out=kpt(uout), in_=acc.rearrange("p (k t) -> p k t", k=KT)), "SO", 16)
                    P.wg(g, "SO")
                P.gpsimd(dbgst)
                _prog_cache[key] = nc
                return nc
            if upto <= 2:
                def dbgst(g):
                    for s_ in list(P.cnt):
                        if s_.startswith("L"):
                            P.wg(g, s_)
                    P.wg(g, "ACC", P.cnt["ACC"])
                    P.inc(g.dma_start(out=kpt(uout), in_=acc.rearrange("p (k t) -> p k t", k=KT)), "SO", 16)
                    P.wg(g, "SO")
                P.gpsimd(dbgst)
                _prog_cache[key] = nc
                return nc

            def shload(g):
                P.wg(g, "UPM", P.cnt["UPM"])
                P.wg(g, "DNM", P.cnt["DNM"])
                P.inc(g.dma_start(out=ws[0].rearrange("p (k m) -> p k m", k=KT),
                                  in_=shup.rearrange("(k p) m -> p k m", p=128)), "Lw0", 16)
                P.inc(g.dma_start(out=ws[1].rearrange("p (k m) -> p k m", k=KT),
                                  in_=shdn.rearrange("(k p) m -> p k m", p=128)), "Lw1", 16)
            P.gpsimd(shload)
            nW += 2

            for mh in range(KT):
                for tc in range(TC):
                    nsh = mh * TC + tc
                    j = nsh % 2

                    def sh_mm(t, mh=mh, tc=tc, j=j, nsh=nsh):
                        P.wg(t, "Lw0", P.cnt["Lw0"])
                        if nsh == 0:
                            P.wg(t, "DNM", P.cnt["DNM"])
                        if nsh >= 2:
                            P.wg(t, "SHS", nsh - 1)
                        for k in range(KT):
                            ins = t.matmul(p_u[j][:, :],
                                           ws[0][:, k * D + mh * 128:k * D + (mh + 1) * 128],
                                           ub[:, k * T + tc * TCW:k * T + (tc + 1) * TCW],
                                           start=(k == 0), stop=(k == KT - 1))
                            if k == KT - 1:
                                P.inc(ins, "SHM")
                    P.tensor(sh_mm)

                    def sh_silu(s_, mh=mh, tc=tc, j=j):
                        P.wg(s_, "SHM", P.cnt["SHM"])
                        P.inc(s_.activation(he[:, mh * T + tc * TCW: mh * T + (tc + 1) * TCW],
                                            p_u[j][:, :], AF.Silu,
                                            bias=prm[:, NE * KT + mh: NE * KT + mh + 1]), "SHS")
                    P.scalar(sh_silu)

            for mo in range(KT):
                for tc in range(TC):
                    nfin = mo * TC + tc
                    j = nfin % 2

                    def xload(q, mo=mo, tc=tc, j=j, nfin=nfin):
                        if nfin >= 2:
                            P.wg(q, "EVF", 2 * (nfin - 1))
                        P.inc(q.dma_start(
                            out=xrs[j][:, :],
                            in_=xin[mo * 128:(mo + 1) * 128, tc * TCW:(tc + 1) * TCW]), f"Lxr{j}", 16)
                    P.sync(xload)

                    def fin_mm(t, mo=mo, tc=tc, j=j, nfin=nfin):
                        P.wg(t, "SHS", P.cnt["SHS"])
                        P.wg(t, "Lw1", P.cnt["Lw1"])
                        if nfin >= 2:
                            P.wg(t, "EVF", 2 * (nfin - 1))
                        for k in range(KT):
                            ins = t.matmul(p_d[j][:, :],
                                           ws[1][:, k * D + mo * 128:k * D + (mo + 1) * 128],
                                           he[:, k * T + tc * TCW:k * T + (tc + 1) * TCW],
                                           start=(k == 0), stop=(k == KT - 1))
                            if k == KT - 1:
                                P.inc(ins, "FIM")
                    P.tensor(fin_mm)

                    def fin_ev(v, mo=mo, tc=tc, j=j, nfin=nfin):
                        P.wg(v, "FIM", P.cnt["FIM"])
                        P.wg(v, f"Lxr{j}", P.cnt[f"Lxr{j}"])
                        P.wg(v, "ACC", P.cnt["ACC"])
                        if nfin >= 2:
                            P.wg(v, f"SO{j}", 16 * (nfin // 2))
                        sl = acc[:, mo * T + tc * TCW: mo * T + (tc + 1) * TCW]
                        P.inc(v.scalar_tensor_tensor(
                            uo[j][:, :], p_d[j][:, :],
                            prm[:, NE * KT + KT + mo: NE * KT + KT + mo + 1],
                            sl, ALU.add, ALU.add), "EVF")
                        P.wg(v, "EVF")
                        P.inc(v.scalar_tensor_tensor(
                            uo[j][:, :], uo[j][:, :], 0.5, xrs[j][:, :],
                            ALU.mult, ALU.add), "EVF")
                    P.vector(fin_ev)

                    def store(g, mo=mo, tc=tc, j=j):
                        P.wg(g, "EVF", P.cnt["EVF"])
                        P.inc(g.dma_start(
                            out=uout[mo * 128:(mo + 1) * 128, tc * TCW:(tc + 1) * TCW],
                            in_=uo[j][:, :]), f"SO{j}", 16)
                    P.gpsimd(store)

            P.gpsimd(lambda g: (P.wg(g, "SO0"), P.wg(g, "SO1")))
        _prog_cache[key] = nc
    return nc


ONES_ROW_BF = np.ones((1, 128), dtype=BF)
SELB = np.zeros((8, 8 * 128), np.float32)
for _e in range(8):
    SELB[_e, _e * 128:(_e + 1) * 128] = 1.0
SELB = SELB.astype(BF)


def pmoe_inputs(ins):
    _, _, _, eye = pa_inds()
    return dict(
        gwT=np.ascontiguousarray(
            np.asarray(ins["L1_ffn_gate_w"], np.float32).T.reshape(KT, 128, NE)
            .transpose(1, 0, 2).reshape(128, KT * NE)),
        upw=np.ascontiguousarray(np.asarray(ins["L1_ffn_up_w"], np.float32).reshape(NE * D, D)),
        dnw=np.ascontiguousarray(np.asarray(ins["L1_ffn_down_w"], np.float32).reshape(NE * D, D)),
        upb=np.ascontiguousarray(np.concatenate(
            [per_part(ins["L1_ffn_up_b"][e]) for e in range(NE)], axis=1)),
        dnb=np.ascontiguousarray(np.asarray(ins["L1_ffn_down_b"], np.float32)),
        shup=np.ascontiguousarray(np.asarray(ins["L1_ffn_sh_up_w"], np.float32)),
        shdn=np.ascontiguousarray(np.asarray(ins["L1_ffn_sh_down_w"], np.float32)),
        shupb=per_part(ins["L1_ffn_sh_up_b"]),
        shdnb=per_part(ins["L1_ffn_sh_down_b"]),
        onrb=ONES_ROW_BF, selb=SELB, eye=eye)


def run_pmoe(u_cores, x_cores, wmap, upto=9):
    nc = build_pmoe(upto)
    in_maps = []
    for c in range(NCORES):
        in_maps.append(dict(uin=np.ascontiguousarray(np.asarray(u_cores[c], np.float32)),
                            xin=np.ascontiguousarray(np.asarray(x_cores[c], np.float32)),
                            **wmap))
    res = run_bass_kernel_spmd(nc, in_maps, core_ids=list(range(NCORES)))
    return [r["uout"] for r in res.results]


DS = 512
DSK = DS // 128
NST = 16
SH = 1024
NQ = SEQ // SH
MTC = 16


def build_pm1():
    if "pm1" in _prog_cache:
        return _prog_cache["pm1"]
    nc = bass.Bass("TRN2", target_bir_lowering=False, debug=False)
    stack = contextlib.ExitStack()
    with stack:
        uin = nc.dram_tensor("uin", [D, SEQ], F32, kind="ExternalInput").ap()
        inwj = nc.dram_tensor("inwj", [D, 2 * DS], F32, kind="ExternalInput").ap()
        xpwj = nc.dram_tensor("xpwj", [DS, 96], F32, kind="ExternalInput").ap()
        cw_d = nc.dram_tensor("cw", [128, DSK * 4], F32, kind="ExternalInput").ap()
        cb_d = nc.dram_tensor("cb", [128, DSK], F32, kind="ExternalInput").ap()
        xcs_o = nc.dram_tensor("xcs", [DS, SEQ], F32, kind="ExternalOutput").ap()
        z_o = nc.dram_tensor("zo", [DS, SEQ], F32, kind="ExternalOutput").ap()
        pdbl_o = nc.dram_tensor("pdbl", [96, SEQ], F32, kind="ExternalOutput").ap()

        ub = [sbt(nc, stack, f"ub{i}", [128, KT * TCW], BF16) for i in range(2)]
        iws = sbt(nc, stack, "iws", [128, KT * 2 * DS], BF16)
        xpws = sbt(nc, stack, "xpws", [128, DSK * 96], BF16)
        xcb = [sbt(nc, stack, f"xcb{i}", [128, DSK * (TCW + 3)], F32) for i in range(2)]
        xss = [sbt(nc, stack, f"xss{i}", [128, DSK * TCW], F32) for i in range(2)]
        xsb = [sbt(nc, stack, f"xsb{i}", [128, DSK * TCW], BF16) for i in range(2)]
        zz = [sbt(nc, stack, f"zz{i}", [128, DSK * TCW], F32) for i in range(2)]
        pdb = [sbt(nc, stack, f"pdb{i}", [96, TCW], F32) for i in range(2)]
        carry = sbt(nc, stack, "carry", [128, DSK * 3], F32)
        cwt = sbt(nc, stack, "cwt", [128, DSK * 4 + DSK], F32)
        p_i = [pst(nc, stack, f"p_i{i}", [128, TCW]) for i in range(2)]
        p_db = pst(nc, stack, "p_db", [96, TCW])

        with nc.Block() as block:
            P = Em(nc, block, stack)

            def consts(q):
                P.inc(q.dma_start(out=cwt[:, 0:DSK * 4], in_=cw_d[:, :]), "Lc", 16)
                P.inc(q.dma_start(out=cwt[:, DSK * 4:DSK * 5], in_=cb_d[:, :]), "Lc", 16)
            P.sync(consts)

            def wcast(g):
                P.inc(g.dma_start(out=iws.rearrange("p (k m) -> p k m", k=KT),
                                  in_=inwj.rearrange("(k p) m -> p k m", p=128)), "Lw", 16)
                P.inc(g.dma_start(out=xpws.rearrange("p (k m) -> p k m", k=DSK),
                                  in_=xpwj.rearrange("(k p) m -> p k m", p=128)), "Lw", 16)
            P.gpsimd(wcast)

            def zcar(v):
                P.inc(v.memset(carry[:, :], 0.0), "CZ")
            P.vector(zcar)

            for tc in range(MTC):
                i = tc % 2

                def uload(g, tc=tc, i=i):
                    if tc >= 2:
                        P.wg(g, "IPM", 8 * (tc - 1))
                    P.inc(g.dma_start(
                        out=ub[i].rearrange("p (k t) -> p k t", k=KT),
                        in_=kpt(uin)[:, :, tc * TCW:(tc + 1) * TCW]), f"Lu{i}", 16)
                P.gpsimd(uload)

                for m in range(8):
                    jb = (tc * 8 + m) % 2

                    def ip_mm(t, tc=tc, i=i, m=m, jb=jb):
                        if tc == 0 and m == 0:
                            P.wg(t, "Lw", 32)
                        P.wg(t, f"Lu{i}", 16 * (tc // 2 + 1))
                        n = tc * 8 + m
                        if n >= 2:
                            P.wg(t, "IPE", n - 1)
                        for k in range(KT):
                            ins = t.matmul(p_i[jb][:, :],
                                           iws[:, k * 2 * DS + m * 128:k * 2 * DS + (m + 1) * 128],
                                           ub[i][:, k * TCW:(k + 1) * TCW],
                                           start=(k == 0), stop=(k == KT - 1))
                            if k == KT - 1:
                                P.inc(ins, "IPM")
                    P.tensor(ip_mm)

                    def ip_ev(v, tc=tc, i=i, m=m, jb=jb):
                        P.wg(v, "IPM", P.cnt["IPM"])
                        if tc >= 2 and m == 0:
                            P.wg(v, f"XSD{tc % 2}", 32 * (tc // 2))
                        if m < DSK:
                            dst = xcb[i][:, m * (TCW + 3) + 3: (m + 1) * (TCW + 3)]
                        else:
                            dst = zz[i][:, (m - DSK) * TCW: (m - DSK + 1) * TCW]
                        P.inc(v.tensor_copy(dst, p_i[jb][:, :]), "IPE")
                    P.vector(ip_ev)

                def conv(v, tc=tc, i=i):
                    P.wg(v, "IPE", P.cnt["IPE"])
                    P.wg(v, "Lc", 32)
                    if tc == 0:
                        P.wg(v, "CZ", 1)
                    if tc >= 2:
                        P.wg(v, f"XSD{tc % 2}", 32 * (tc // 2))
                    for kd in range(DSK):
                        base = kd * (TCW + 3)
                        v.tensor_copy(xcb[i][:, base:base + 3], carry[:, kd * 3:(kd + 1) * 3])
                    P.inc(v.engine_nop(), "CVH")
                    P.wg(v, "CVH")
                    for kd in range(DSK):
                        base = kd * (TCW + 3)
                        dst = xss[i][:, kd * TCW:(kd + 1) * TCW]
                        src = lambda tap: xcb[i][:, base + tap: base + tap + TCW]
                        P.inc(v.tensor_scalar_mul(dst, src(0), cwt[:, kd * 4:kd * 4 + 1]), "CVS")
                        for tap in range(1, 4):
                            P.wg(v, "CVS")
                            P.inc(v.scalar_tensor_tensor(
                                dst, src(tap), cwt[:, kd * 4 + tap:kd * 4 + tap + 1],
                                dst, ALU.mult, ALU.add), "CVS")
                        P.wg(v, "CVS")
                        ins = v.tensor_copy(carry[:, kd * 3:(kd + 1) * 3],
                                            xcb[i][:, base + TCW: base + TCW + 3])
                        if kd == DSK - 1:
                            P.inc(ins, "CVD")
                P.vector(conv)

                def csilu(s_, tc=tc, i=i):
                    P.wg(s_, "CVD", P.cnt["CVD"])
                    if tc >= 2:
                        P.wg(s_, "PDM", tc - 1)
                    for kd in range(DSK):
                        sl = xss[i][:, kd * TCW:(kd + 1) * TCW]
                        ins = s_.activation(sl, sl, AF.Silu, bias=cwt[:, DSK * 4 + kd:DSK * 4 + kd + 1])
                        if kd == DSK - 1:
                            P.inc(ins, "CSD")
                    P.wg(s_, "CSD", P.cnt["CSD"])
                    for kd in range(DSK):
                        ins = s_.copy(xsb[i][:, kd * TCW:(kd + 1) * TCW],
                                      xss[i][:, kd * TCW:(kd + 1) * TCW])
                        if kd == DSK - 1:
                            P.inc(ins, "CBD")
                P.scalar(csilu)

                def pd_mm(t, tc=tc, i=i):
                    P.wg(t, "CBD", P.cnt["CBD"])
                    if tc >= 2:
                        P.wg(t, "PDE", tc - 1)
                    for kd in range(DSK):
                        ins = t.matmul(p_db[:, :], xpws[:, kd * 96:(kd + 1) * 96],
                                       xsb[i][:, kd * TCW:(kd + 1) * TCW],
                                       start=(kd == 0), stop=(kd == DSK - 1))
                        if kd == DSK - 1:
                            P.inc(ins, "PDM")
                P.tensor(pd_mm)

                def pd_ev(v, tc=tc, i=i):
                    P.wg(v, "PDM", P.cnt["PDM"])
                    if tc >= 2:
                        P.wg(v, f"PSD{tc % 2}", 16 * (tc // 2))
                    P.inc(v.tensor_copy(pdb[i][:, :], p_db[0:96, :]), "PDE")
                P.vector(pd_ev)

                def stores(g, tc=tc, i=i):
                    P.wg(g, "CSD", P.cnt["CSD"])
                    P.wg(g, "IPE", P.cnt["IPE"])
                    P.wg(g, "PDE", P.cnt["PDE"])
                    P.inc(g.dma_start(
                        out=xcs_o.rearrange("(k p) t -> p k t", p=128)[:, :, tc * TCW:(tc + 1) * TCW],
                        in_=xss[i].rearrange("p (k t) -> p k t", k=DSK)), f"XSD{tc % 2}", 16)
                    P.inc(g.dma_start(
                        out=z_o.rearrange("(k p) t -> p k t", p=128)[:, :, tc * TCW:(tc + 1) * TCW],
                        in_=zz[i].rearrange("p (k t) -> p k t", k=DSK)), f"XSD{tc % 2}", 16)
                    P.inc(g.dma_start(
                        out=pdbl_o[:, tc * TCW:(tc + 1) * TCW], in_=pdb[i][:, :]), f"PSD{tc % 2}", 16)
                P.gpsimd(stores)

            def fin(g):
                for s_ in ("XSD0", "XSD1", "PSD0", "PSD1"):
                    P.wg(g, s_)
            P.gpsimd(fin)
        _prog_cache["pm1"] = nc
    return nc


def build_pm2():
    if "pm2" in _prog_cache:
        return _prog_cache["pm2"]
    nc = bass.Bass("TRN2", target_bir_lowering=False, debug=False)
    stack = contextlib.ExitStack()
    with stack:
        xcs_i = nc.dram_tensor("xcs", [DS, SEQ], F32, kind="ExternalInput").ap()
        z_i = nc.dram_tensor("zo", [DS, SEQ], F32, kind="ExternalInput").ap()
        pd = [nc.dram_tensor(f"pd{r}", [96, SEQ], F32, kind="ExternalInput").ap()
              for r in range(4)]
        dtwj = nc.dram_tensor("dtwj", [64, DS], F32, kind="ExternalInput").ap()
        dtb_d = nc.dram_tensor("dtb", [128, DSK], F32, kind="ExternalInput").ap()
        A_d = nc.dram_tensor("Ad", [128, DSK * NST], F32, kind="ExternalInput").ap()
        Dp_d = nc.dram_tensor("Dp", [128, DSK], F32, kind="ExternalInput").ap()
        sel16_d = nc.dram_tensor("sel16", [16, 16 * 128], F32, kind="ExternalInput").ap()
        onec_d = nc.dram_tensor("onec", [128, 1], F32, kind="ExternalInput").ap()
        negc_d = nc.dram_tensor("negc", [128, 1], F32, kind="ExternalInput").ap()
        g_o = nc.dram_tensor("go", [DS, SEQ], F32, kind="ExternalOutput").ap()

        dbl = sbt(nc, stack, "dbl", [96, SEQ], F32)
        dblx = sbt(nc, stack, "dblx", [96, SEQ], F32)
        dtws = sbt(nc, stack, "dtws", [64, DS], BF16)
        dblb = sbt(nc, stack, "dblb", [64, SEQ], BF16)
        Brow = sbt(nc, stack, "Brow", [16, SEQ], F32)
        Crow = sbt(nc, stack, "Crow", [16, SEQ], F32)
        prm = sbt(nc, stack, "prm", [128, DSK * NST + 2 * DSK + 2], F32)
        sel16 = sbt(nc, stack, "sel16_s", [16, 16 * 128], F32)
        delta = sbt(nc, stack, "delta", [128, SH], F32)
        xq = sbt(nc, stack, "xq", [128, SH], F32)
        uq = sbt(nc, stack, "uq", [128, SH], F32)
        yac = sbt(nc, stack, "yac", [128, SH], F32)
        dA = sbt(nc, stack, "dA", [128, SH], F32)
        dBx = sbt(nc, stack, "dBx", [128, SH], F32)
        hsc = sbt(nc, stack, "hsc", [128, SH], F32)
        bcs = sbt(nc, stack, "bcs", [128, 2 * SH], F32)
        zq = sbt(nc, stack, "zq", [128, SH], F32)
        carry = sbt(nc, stack, "carry", [128, DSK * NST], F32)
        abs_s = sbt(nc, stack, "abs_s", [128, TCW], F32)
        p_dl = [pst(nc, stack, f"p_dl{i}", [128, TCW]) for i in range(2)]
        p_bc = [pst(nc, stack, f"p_bc{i}", [128, TCW]) for i in range(2)]

        AOF = 0
        BOF = DSK * NST
        DOF = DSK * NST + DSK
        OOF = DSK * NST + 2 * DSK

        with nc.Block() as block:
            P = Em(nc, block, stack)

            def consts(q_):
                P.inc(q_.dma_start(out=prm[:, AOF:AOF + DSK * NST], in_=A_d[:, :]), "Lc", 16)
                P.inc(q_.dma_start(out=prm[:, BOF:BOF + DSK], in_=dtb_d[:, :]), "Lc", 16)
                P.inc(q_.dma_start(out=prm[:, DOF:DOF + DSK], in_=Dp_d[:, :]), "Lc", 16)
                P.inc(q_.dma_start(out=prm[:, OOF:OOF + 1], in_=onec_d[:, :]), "Lc", 16)
                P.inc(q_.dma_start(out=prm[:, OOF + 1:OOF + 2], in_=negc_d[:, :]), "Lc", 16)
                P.inc(q_.dma_start(out=sel16[:, :], in_=sel16_d[:, :]), "Lc", 16)
            P.sync(consts)

            def ld01(g):
                P.inc(g.dma_start(out=dbl[:, :], in_=pd[0][:, :]), "Lp0", 16)
                P.inc(g.dma_start(out=dblx[:, :], in_=pd[1][:, :]), "Lp1", 16)
            P.gpsimd(ld01)
            for r in (1, 2, 3):
                def sumd(v, r=r):
                    P.wg(v, f"Lp{r}", 16)
                    if r == 1:
                        P.wg(v, "Lp0", 16)
                    P.inc(v.tensor_add(dbl[:, :], dbl[:, :], dblx[:, :]), "SDB")
                P.vector(sumd)
                if r < 3:
                    def ldn(g, r=r):
                        P.wg(g, "SDB", P.cnt["SDB"])
                        P.inc(g.dma_start(out=dblx[:, :], in_=pd[r + 1][:, :]), f"Lp{r + 1}", 16)
                    P.gpsimd(ldn)

            def bcmove(g):
                P.wg(g, "SDB", 3)
                P.inc(g.dma_start(out=dblb[:, :], in_=dbl[0:64, :]), "BCM", 16)
                P.inc(g.dma_start(out=Brow[:, :], in_=dbl[64:80, :]), "BCM", 16)
                P.inc(g.dma_start(out=Crow[:, :], in_=dbl[80:96, :]), "BCM", 16)
                P.inc(g.dma_start(out=dtws[:, :], in_=dtwj[:, :]), "Lw", 16)
            P.gpsimd(bcmove)

            def zcar(v):
                P.inc(v.memset(carry[:, :], 0.0), "CZ")
            P.vector(zcar)

            nun = 0
            for q in range(NQ):
                for kd in range(DSK):
                    for t2 in range(SH // TCW):
                        nd = (q * DSK + kd) * (SH // TCW) + t2
                        jb = nd % 2

                        def dl_mm(t, q=q, kd=kd, t2=t2, jb=jb, nd=nd):
                            if nd == 0:
                                P.wg(t, "Lw", 16)
                                P.wg(t, "BCM", 48)
                            if nd >= 2:
                                P.wg(t, "SPr", nd - 1)
                            rhs = dblb[:, q * SH + t2 * TCW: q * SH + (t2 + 1) * TCW]
                            P.inc(t.matmul(p_dl[jb][:, :], dtws[:, kd * 128:(kd + 1) * 128],
                                           rhs, start=True, stop=True), "DLM")
                        P.tensor(dl_mm)

                        def sp_v1(v, kd=kd, t2=t2, jb=jb, nd=nd):
                            if nd == 0:
                                P.wg(v, "Lc", P.cnt["Lc"])
                            P.wg(v, "DLM", P.cnt["DLM"])
                            if nd >= 1:
                                P.wg(v, "SPl", P.cnt["SPl"])
                            P.inc(v.tensor_scalar_add(abs_s[:, :], p_dl[jb][:, :],
                                                      prm[:, BOF + kd:BOF + kd + 1]), "SPa")
                        P.vector(sp_v1)

                        def sp_s1(s_, t2=t2, nd=nd):
                            if nd == 0:
                                P.wg(s_, "Lc", P.cnt["Lc"])
                            P.wg(s_, "SPa", P.cnt["SPa"])
                            dst = delta[:, t2 * TCW:(t2 + 1) * TCW]
                            P.inc(s_.activation(dst, abs_s[:, :], AF.Relu), "SPr")
                            P.wg(s_, "SPr")
                            P.inc(s_.activation(abs_s[:, :], abs_s[:, :], AF.Abs), "SPe")
                            P.wg(s_, "SPe")
                            P.inc(s_.activation(abs_s[:, :], abs_s[:, :], AF.Exp,
                                                scale=prm[:, OOF + 1:OOF + 2]), "SPe")
                            P.wg(s_, "SPe")
                            P.inc(s_.activation(abs_s[:, :], abs_s[:, :], AF.Ln,
                                                bias=prm[:, OOF:OOF + 1]), "SPe")
                        P.scalar(sp_s1)

                        def sp_v2(v, t2=t2):
                            P.wg(v, "SPe", P.cnt["SPe"])
                            P.wg(v, "SPr", P.cnt["SPr"])
                            dst = delta[:, t2 * TCW:(t2 + 1) * TCW]
                            P.inc(v.tensor_add(dst, dst, abs_s[:, :]), "SPl")
                        P.vector(sp_v2)

                    def xload(q_, q=q, kd=kd):
                        if q + kd > 0:
                            P.wg(q_, "GST", P.cnt["GST"])
                        P.inc(q_.dma_start(
                            out=xq[:, :],
                            in_=xcs_i[kd * 128:(kd + 1) * 128, q * SH:(q + 1) * SH]), "LxQ", 16)
                        P.inc(q_.dma_start(
                            out=zq[:, :],
                            in_=z_i[kd * 128:(kd + 1) * 128, q * SH:(q + 1) * SH]), "LxQ", 16)
                    P.sync(xload)

                    def prep(v, q=q, kd=kd):
                        P.wg(v, "SPl", P.cnt["SPl"])
                        P.wg(v, "LxQ", P.cnt["LxQ"])
                        if q + kd > 0:
                            P.wg(v, "GMU", P.cnt.get("GMU", 0))
                        P.inc(v.tensor_mul(uq[:, :], delta[:, :], xq[:, :]), "UQD")
                        P.inc(v.memset(yac[:, :], 0.0), "YZ")
                    P.vector(prep)

                    for n in range(NST):
                        jb = nun % 2

                        def bc_mm(t, q=q, n=n, jb=jb, nun=nun):
                            if nun >= 1:
                                P.wg(t, "BCE", P.cnt["BCE"])
                            for t2 in range(SH // TCW):
                                rhs_b = Brow[:, q * SH + t2 * TCW: q * SH + (t2 + 1) * TCW]
                                P.inc(t.matmul(p_bc[0][:, :], sel16[:, n * 128:(n + 1) * 128],
                                               rhs_b, start=True, stop=True), "BCMM")
                                rhs_c = Crow[:, q * SH + t2 * TCW: q * SH + (t2 + 1) * TCW]
                                P.inc(t.matmul(p_bc[1][:, :], sel16[:, n * 128:(n + 1) * 128],
                                               rhs_c, start=True, stop=True), "BCMM")
                                P.wg(t, "BCE", P.cnt.get("BCE", 0) - (0 if t2 == (SH // TCW) - 1 else -0))
                        P.tensor(bc_mm) if False else None

                        for t2 in range(SH // TCW):
                            def bc1(t, q=q, n=n, t2=t2, nun=nun):
                                if nun == 0 and t2 == 0:
                                    P.wg(t, "Lc", P.cnt["Lc"])
                                P.wg(t, "BCE", P.cnt.get("BCE", 0))
                                rhs_b = Brow[:, q * SH + t2 * TCW: q * SH + (t2 + 1) * TCW]
                                P.inc(t.matmul(p_bc[0][:, :], sel16[:, n * 128:(n + 1) * 128],
                                               rhs_b, start=True, stop=True), "BCMM")
                                rhs_c = Crow[:, q * SH + t2 * TCW: q * SH + (t2 + 1) * TCW]
                                P.inc(t.matmul(p_bc[1][:, :], sel16[:, n * 128:(n + 1) * 128],
                                               rhs_c, start=True, stop=True), "BCMM")
                            P.tensor(bc1)

                            def bc2(s_, t2=t2):
                                P.wg(s_, "BCMM", P.cnt["BCMM"])
                                s_.copy(bcs[:, t2 * TCW:(t2 + 1) * TCW], p_bc[0][:, :])
                                P.inc(s_.copy(bcs[:, SH + t2 * TCW: SH + (t2 + 1) * TCW],
                                              p_bc[1][:, :]), "BCE")
                            P.scalar(bc2)

                        def s_dA(s_, kd=kd, n=n):
                            P.wg(s_, "SPl", P.cnt["SPl"])
                            if nun >= 1:
                                P.wg(s_, "SCN", P.cnt["SCN"])
                            P.inc(s_.activation(dA[:, :], delta[:, :], AF.Exp,
                                                scale=prm[:, AOF + kd * NST + n: AOF + kd * NST + n + 1]),
                                  "DAD")
                        P.scalar(s_dA)

                        def v_scan(v, kd=kd, n=n, nun=nun):
                            P.wg(v, "UQD", P.cnt["UQD"])
                            P.wg(v, "BCE", P.cnt["BCE"])
                            P.wg(v, "DAD", P.cnt["DAD"])
                            P.inc(v.tensor_mul(dBx[:, :], uq[:, :], bcs[:, 0:SH]), "DBD")
                            P.wg(v, "DBD")
                            P.inc(v.tensor_tensor_scan(hsc[:, :], dA[:, :], dBx[:, :],
                                                       carry[:, kd * NST + n: kd * NST + n + 1],
                                                       ALU.mult, ALU.add), "SCN")
                            P.wg(v, "SCN")
                            P.inc(v.tensor_copy(carry[:, kd * NST + n: kd * NST + n + 1],
                                                hsc[:, SH - 1:SH]), "CRY")
                            P.wg(v, "CRY")
                            P.inc(v.tensor_mul(hsc[:, :], hsc[:, :], bcs[:, SH:2 * SH]), "YM1")
                            P.wg(v, "YM1")
                            P.inc(v.tensor_add(yac[:, :], yac[:, :], hsc[:, :]), "YAD")
                        P.vector(v_scan)
                        nun += 1

                    def fin_v(v, kd=kd):
                        P.wg(v, "YAD", P.cnt["YAD"])
                        P.inc(v.scalar_tensor_tensor(yac[:, :], xq[:, :],
                                                     prm[:, DOF + kd:DOF + kd + 1],
                                                     yac[:, :], ALU.mult, ALU.add), "YDD")
                    P.vector(fin_v)

                    def fin_s(s_):
                        P.wg(s_, "LxQ", P.cnt["LxQ"])
                        P.wg(s_, "GMU", P.cnt.get("GMU", 0))
                        P.inc(s_.activation(zq[:, :], zq[:, :], AF.Silu), "SZD")
                    P.scalar(fin_s)

                    def fin_v2(v):
                        P.wg(v, "YDD", P.cnt["YDD"])
                        P.wg(v, "SZD", P.cnt["SZD"])
                        P.inc(v.tensor_mul(yac[:, :], yac[:, :], zq[:, :]), "GMU")
                    P.vector(fin_v2)

                    def gst(g, q=q, kd=kd):
                        P.wg(g, "GMU", P.cnt["GMU"])
                        P.inc(g.dma_start(
                            out=g_o[kd * 128:(kd + 1) * 128, q * SH:(q + 1) * SH],
                            in_=yac[:, :]), "GST", 16)
                    P.gpsimd(gst)

            def fini(g):
                P.wg(g, "GST")
                P.wg(g, "CRY")
            P.gpsimd(fini)
        _prog_cache["pm2"] = nc
    return nc


def build_pm3():
    if "pm3" in _prog_cache:
        return _prog_cache["pm3"]
    nc = bass.Bass("TRN2", target_bir_lowering=False, debug=False)
    stack = contextlib.ExitStack()
    with stack:
        xin = nc.dram_tensor("xin", [D, T], F32, kind="ExternalInput").ap()
        uin = nc.dram_tensor("uin", [D, T], F32, kind="ExternalInput").ap()
        gall = nc.dram_tensor("gall", [2 * D, T], F32, kind="ExternalInput").ap()
        opw = nc.dram_tensor("opw", [2 * D, D], F32, kind="ExternalInput").ap()
        xout = nc.dram_tensor("xout", [D, T], F32, kind="ExternalOutput").ap()

        gb = sbt(nc, stack, "gb", [128, 16 * T], BF16)
        ow = sbt(nc, stack, "ow", [128, 16 * D], BF16)
        xrs = [sbt(nc, stack, f"xrs{i}", [128, TCW], F32) for i in range(2)]
        urs = [sbt(nc, stack, f"urs{i}", [128, TCW], F32) for i in range(2)]
        uo = [sbt(nc, stack, f"uo{i}", [128, TCW], F32) for i in range(2)]
        p_o = [pst(nc, stack, f"p_o{i}", [128, TCW]) for i in range(2)]

        with nc.Block() as block:
            P = Em(nc, block, stack)

            def casts(g):
                P.inc(g.dma_start(out=gb.rearrange("p (k t) -> p k t", k=16),
                                  in_=gall.rearrange("(k p) t -> p k t", p=128)), "Lg", 16)
                P.inc(g.dma_start(out=ow.rearrange("p (k m) -> p k m", k=16),
                                  in_=opw.rearrange("(k p) m -> p k m", p=128)), "Lw", 16)
            P.gpsimd(casts)

            for mo in range(KT):
                for tc in range(TC):
                    n = mo * TC + tc
                    j = n % 2

                    def rload(q_, mo=mo, tc=tc, j=j, n=n):
                        if n >= 2:
                            P.wg(q_, "EV", 2 * (n - 1))
                        P.inc(q_.dma_start(
                            out=xrs[j][:, :],
                            in_=xin[mo * 128:(mo + 1) * 128, tc * TCW:(tc + 1) * TCW]), f"Lr{j}", 16)
                        P.inc(q_.dma_start(
                            out=urs[j][:, :],
                            in_=uin[mo * 128:(mo + 1) * 128, tc * TCW:(tc + 1) * TCW]), f"Lr{j}", 16)
                    P.sync(rload)

                    def o_mm(t, mo=mo, tc=tc, j=j, n=n):
                        if n == 0:
                            P.wg(t, "Lg", 16)
                            P.wg(t, "Lw", 16)
                        if n >= 2:
                            P.wg(t, "EV", 2 * (n - 1))
                        for k in range(16):
                            ins = t.matmul(p_o[j][:, :], ow[:, k * D + mo * 128:k * D + (mo + 1) * 128],
                                           gb[:, k * T + tc * TCW:k * T + (tc + 1) * TCW],
                                           start=(k == 0), stop=(k == 15))
                            if k == 15:
                                P.inc(ins, "OM")
                    P.tensor(o_mm)

                    def evac(v, j=j, n=n):
                        P.wg(v, "OM", P.cnt["OM"])
                        P.wg(v, f"Lr{j}", P.cnt[f"Lr{j}"])
                        if n >= 2:
                            P.wg(v, f"SO{j}", 16 * (n // 2))
                        P.inc(v.tensor_add(uo[j][:, :], p_o[j][:, :], xrs[j][:, :]), "EV")
                        P.wg(v, "EV")
                        P.inc(v.tensor_add(uo[j][:, :], uo[j][:, :], urs[j][:, :]), "EV")
                    P.vector(evac)

                    def store(g, mo=mo, tc=tc, j=j):
                        P.wg(g, "EV", P.cnt["EV"])
                        P.inc(g.dma_start(
                            out=xout[mo * 128:(mo + 1) * 128, tc * TCW:(tc + 1) * TCW],
                            in_=uo[j][:, :]), f"SO{j}", 16)
                    P.gpsimd(store)

            P.gpsimd(lambda g: (P.wg(g, "SO0"), P.wg(g, "SO1")))
        _prog_cache["pm3"] = nc
    return nc


SEL16 = np.zeros((16, 16 * 128), np.float32)
for _n in range(16):
    SEL16[_n, _n * 128:(_n + 1) * 128] = 1.0
NEG_COL = np.full((128, 1), -1.0, np.float32)
ONEC = np.ones((128, 1), np.float32)


def mamba_weight_maps(ins):
    inwT = np.asarray(ins["L0_attn_in_proj_w"], np.float32).T
    xpwT = np.asarray(ins["L0_attn_x_proj_w"], np.float32).T
    cw = np.asarray(ins["L0_attn_conv_w"], np.float32)[:, 0, :]
    cb = np.asarray(ins["L0_attn_conv_b"], np.float32)
    dtw = np.asarray(ins["L0_attn_dt_proj_w"], np.float32)
    dtb = np.asarray(ins["L0_attn_dt_proj_b"], np.float32)
    Al = np.asarray(ins["L0_attn_A_log"], np.float32)
    Dp = np.asarray(ins["L0_attn_D_param"], np.float32)
    opwT = np.asarray(ins["L0_attn_out_proj_w"], np.float32).T
    m1, m2 = [], []
    for j in range(4):
        sl = slice(j * DS, (j + 1) * DS)
        inwj = np.concatenate([inwT[:, sl], inwT[:, 2048 + j * DS: 2048 + (j + 1) * DS]], axis=1)
        cwj = cw[sl]
        cw_pp = np.zeros((128, DSK * 4), np.float32)
        for kd in range(DSK):
            for tap in range(4):
                cw_pp[:, kd * 4 + tap] = cwj[kd * 128:(kd + 1) * 128, tap]
        A = -np.exp(Al[sl])
        A_pp = np.zeros((128, DSK * NST), np.float32)
        for kd in range(DSK):
            A_pp[:, kd * NST:(kd + 1) * NST] = A[kd * 128:(kd + 1) * 128]
        m1.append(dict(inwj=np.ascontiguousarray(inwj),
                       xpwj=np.ascontiguousarray(xpwT[sl]),
                       cw=cw_pp, cb=per_part(cb[sl])))
        m2.append(dict(dtwj=np.ascontiguousarray(dtw[sl].T),
                       dtb=per_part(dtb[sl]), Ad=A_pp, Dp=per_part(Dp[sl]),
                       sel16=SEL16, onec=ONEC, negc=NEG_COL))
    return m1, m2, dict(opw=np.ascontiguousarray(opwT))


LC = 128
HG = 4


def build_pr1():
    if "pr1" in _prog_cache:
        return _prog_cache["pr1"]
    nc = bass.Bass("TRN2", target_bir_lowering=False, debug=False)
    stack = contextlib.ExitStack()
    with stack:
        uin = nc.dram_tensor("uin", [D, SEQ], F32, kind="ExternalInput").ap()
        wqT = nc.dram_tensor("wqT", [D, 256], F32, kind="ExternalInput").ap()
        wkT = nc.dram_tensor("wkT", [D, 256], F32, kind="ExternalInput").ap()
        wvT = nc.dram_tensor("wvT", [D, 256], F32, kind="ExternalInput").ap()
        ln1 = nc.dram_tensor("ln1", [128, KT], F32, kind="ExternalInput").ap()
        gam = nc.dram_tensor("gam", [1, HG], F32, kind="ExternalInput").ap()
        ptile = nc.dram_tensor("ptile", [128, 128], F32, kind="ExternalInput").ap()
        iof = nc.dram_tensor("iof", [1, 2 * LC + 1], F32, kind="ExternalInput").ap()
        onr_d = nc.dram_tensor("onr", [1, 128], F32, kind="ExternalInput").ap()
        onc_d = nc.dram_tensor("onc", [128, 1], F32, kind="ExternalInput").ap()
        eps_d = nc.dram_tensor("eps", [1, 1], F32, kind="ExternalInput").ap()
        eye_d = nc.dram_tensor("eye", [128, 128], BF16, kind="ExternalInput").ap()
        indh_d = nc.dram_tensor("indh", [128, 2], F32, kind="ExternalInput").ap()
        yout = nc.dram_tensor("yout", [256, SEQ], F32, kind="ExternalOutput").ap()
        gnp_o = nc.dram_tensor("gnp", [2, 2 * HG], F32, kind="ExternalOutput").ap()

        ut = [sbt(nc, stack, f"ut{i}", [128, KT * TCW], F32) for i in range(2)]
        sqf = sbt(nc, stack, "sqf", [128, KT * TCW], F32)
        hb = sbt(nc, stack, "hb", [128, KT * TCW], BF16)
        wq = sbt(nc, stack, "wq_s", [128, KT * 256], BF16)
        wk = sbt(nc, stack, "wk_s", [128, KT * 256], BF16)
        wv = sbt(nc, stack, "wv_s", [128, KT * 256], BF16)
        qb = sbt(nc, stack, "qb", [128, 2 * TCW], BF16)
        kb = sbt(nc, stack, "kb", [128, 2 * TCW], BF16)
        vb = sbt(nc, stack, "vb", [128, 2 * TCW], BF16)
        kdec = sbt(nc, stack, "kdec", [128, LC], BF16)
        qdec = sbt(nc, stack, "qdec", [128, LC], BF16)
        ktT = sbt(nc, stack, "ktT", [128, 128], BF16)
        vtT = sbt(nc, stack, "vtT", [128, 128], BF16)
        att = sbt(nc, stack, "att", [128, LC], BF16)
        atT = sbt(nc, stack, "atT", [128, LC], BF16)
        dmk = sbt(nc, stack, "dmk", [128, HG * LC], F32)
        gfw = sbt(nc, stack, "gfw", [128, 2 * LC], F32)
        grv = sbt(nc, stack, "grv", [128, 2 * LC], F32)
        glc = sbt(nc, stack, "glc", [128, 2], F32)
        sest = sbt(nc, stack, "sest", [128, 2 * 64], F32)
        sesb = sbt(nc, stack, "sesb", [128, 2 * 64], BF16)
        grow = sbt(nc, stack, "grow", [1, 4 * HG + 4], F32)
        yb = [sbt(nc, stack, f"yb{i}", [128, 2 * TCW], F32) for i in range(2)]
        gns = sbt(nc, stack, "gns", [128, 4], F32)
        gno = sbt(nc, stack, "gno", [2, 2 * HG], F32)
        srow = sbt(nc, stack, "srow", [1, 2 * LC + 4], F32)
        prm = sbt(nc, stack, "prm", [128, KT], F32)
        onr = sbt(nc, stack, "onr_s", [1, 128], F32)
        onc = sbt(nc, stack, "onc_s", [128, 1], F32)
        epst = sbt(nc, stack, "epst_s", [1, 1], F32)
        eye = sbt(nc, stack, "eye_s", [128, 128], BF16)
        indh = sbt(nc, stack, "indh_s", [128, 2], F32)
        ptl = sbt(nc, stack, "ptl_s", [128, 128], F32)
        iot = sbt(nc, stack, "iot_s", [1, 2 * LC + 1], F32)
        rrow = sbt(nc, stack, "rrow", [1, TCW], F32)
        lgc = sbt(nc, stack, "lgc", [128, HG], F32)

        p_a = [pst(nc, stack, f"p_a{i}", [128, TCW]) for i in range(2)]
        p_q = pst(nc, stack, "p_q", [1, TCW])
        p_rb = pst(nc, stack, "p_rb", [128, TCW])
        p_bt = pst(nc, stack, "p_bt", [128, 1024], BF16)
        p_o = pst(nc, stack, "p_o", [128, LC])
        p_st = pst(nc, stack, "p_st", [128, 128])

        with nc.Block() as block:
            P = Em(nc, block, stack)

            def consts(q_):
                for dst, srcd in ((prm, ln1), (onr, onr_d), (onc, onc_d),
                                  (epst, eps_d), (eye, eye_d),
                                  (indh, indh_d), (ptl, ptile), (iot, iof)):
                    P.inc(q_.dma_start(out=dst[:, :], in_=srcd[:, :]), "Lc", 16)
                P.inc(q_.dma_start(out=grow[0:1, 0:HG], in_=gam[:, :]), "Lc", 16)
            P.sync(consts)
            n_const = P.cnt["Lc"]

            def wcast(g):
                for dst, srcd in ((wq, wqT), (wk, wkT), (wv, wvT)):
                    P.inc(g.dma_start(out=dst.rearrange("p (k m) -> p k m", k=KT),
                                      in_=srcd.rearrange("(k p) m -> p k m", p=128)), "Lw", 16)
            P.gpsimd(wcast)

            def gprep_s(s_):
                P.wg(s_, "Lc", n_const)
                P.inc(s_.activation(grow[0:1, HG:2 * HG], grow[0:1, 0:HG], AF.Sigmoid), "GP")
                P.wg(s_, "GP")
                P.inc(s_.activation(grow[0:1, 2 * HG:3 * HG], grow[0:1, HG:2 * HG], AF.Ln), "GP")
            P.scalar(gprep_s)

            def lgbc(t):
                P.wg(t, "GP", P.cnt["GP"])
                P.wg(t, "Lc", n_const)
                P.inc(t.matmul(p_rb[:, 0:HG], onr[:, :], grow[0:1, 2 * HG:3 * HG],
                               start=True, stop=True), "LGB")
            P.tensor(lgbc)

            def lgev(v):
                P.wg(v, "LGB", 1)
                P.inc(v.tensor_copy(lgc[:, :], p_rb[:, 0:HG]), "LGE")
            P.vector(lgev)

            for h in range(HG):
                hp, r0 = h // 2, (h % 2) * 64

                def gprep_h(s_, h=h):
                    lg = grow[0:1, 2 * HG + h:2 * HG + h + 1]
                    if h == 0:
                        P.wg(s_, "LGE", 1)
                    else:
                        P.wg(s_, "GBM", P.cnt["GBM"])
                    P.inc(s_.activation(dmk[:, h * LC:(h + 1) * LC], ptl[:, :], AF.Exp,
                                        scale=lgc[:, h:h + 1]), "DM")
                    P.inc(s_.activation(srow[0:1, 0:LC], iot[0:1, 0:LC], AF.Exp, scale=lg), "DM")
                    P.inc(s_.activation(srow[0:1, LC:2 * LC], iot[0:1, LC:2 * LC], AF.Exp,
                                        scale=lg), "DM")
                    P.inc(s_.activation(srow[0:1, 2 * LC + h:2 * LC + h + 1],
                                        iot[0:1, 2 * LC:2 * LC + 1], AF.Exp, scale=lg), "DM")
                P.scalar(gprep_h)

                def gbc(t, h=h, hp=hp, r0=r0):
                    P.wg(t, "DM", 4 * (h + 1))
                    P.wg(t, "LGE", 1)
                    if h >= 1:
                        P.wg(t, "GBE", P.cnt["GBE"])
                    t.matmul(p_rb[:, 0:LC], onr[:, :], srow[0:1, 0:LC], start=True, stop=True)
                    t.matmul(p_rb[:, LC:2 * LC], onr[:, :], srow[0:1, LC:2 * LC],
                             start=True, stop=True)
                    P.inc(t.matmul(p_rb[:, 2 * LC:2 * LC + 1], onr[:, :],
                                   srow[0:1, 2 * LC + h:2 * LC + h + 1],
                                   start=True, stop=True), "GBM")
                P.tensor(gbc)

                def gbe(v, h=h, hp=hp, r0=r0):
                    P.wg(v, "GBM", P.cnt["GBM"])
                    v.tensor_copy(gfw[r0:r0 + 64, hp * LC:(hp + 1) * LC], p_rb[r0:r0 + 64, 0:LC])
                    v.tensor_copy(grv[r0:r0 + 64, hp * LC:(hp + 1) * LC],
                                  p_rb[r0:r0 + 64, LC:2 * LC])
                    P.inc(v.tensor_copy(glc[r0:r0 + 64, hp:hp + 1],
                                        p_rb[r0:r0 + 64, 2 * LC:2 * LC + 1]), "GBE")
                P.vector(gbe)

            def zinit(v):
                v.memset(sest[:, :], 0.0)
                v.memset(sesb[:, :], 0.0)
                P.inc(v.memset(gns[:, :], 0.0), "ZI")
            P.vector(zinit)

            for tc in range(SEQ // TCW):
                i = tc % 2

                def uload(q_, tc=tc, i=i):
                    if tc >= 2:
                        P.wg(q_, "HBD", 2 * (tc - 1))
                    P.inc(q_.dma_start(
                        out=ut[i].rearrange("p (k t) -> p k t", k=KT),
                        in_=kpt(uin)[:, :, tc * TCW:(tc + 1) * TCW]), f"Lu{i}", 16)
                P.sync(uload)

                def squares(s_, tc=tc, i=i):
                    P.wg(s_, f"Lu{i}", 16 * (tc // 2 + 1))
                    if tc >= 1:
                        P.wg(s_, "HBD", 2 * tc)
                        P.wg(s_, "GSb", P.cnt.get("GSb", 0))
                    for k in range(KT):
                        ins = s_.activation(sqf[:, k * TCW:(k + 1) * TCW],
                                            ut[i][:, k * TCW:(k + 1) * TCW], AF.Square)
                        if k == KT - 1:
                            P.inc(ins, "SQ")
                P.scalar(squares)

                def stats_mm(t, tc=tc):
                    if tc == 0:
                        P.wg(t, "Lc", n_const)
                    P.wg(t, "SQ", P.cnt["SQ"])
                    if tc >= 1:
                        P.wg(t, "VM", tc)
                    for k in range(KT):
                        ins = t.matmul(p_q[:, :], onc[:, :], sqf[:, k * TCW:(k + 1) * TCW],
                                       start=(k == 0), stop=(k == KT - 1))
                        if k == KT - 1:
                            P.inc(ins, "MQ")
                P.tensor(stats_mm)

                def stats_v(v, tc=tc):
                    P.wg(v, "MQ", P.cnt["MQ"])
                    P.inc(v.tensor_scalar_mul(rrow[0:1, :], p_q[0:1, :], 1.0 / D), "VM")
                P.vector(stats_v)

                def stats_s(s_, tc=tc):
                    P.wg(s_, "VM", P.cnt["VM"])
                    P.inc(s_.activation(rrow[0:1, :], rrow[0:1, :], AF.Sqrt,
                                        bias=epst[0:1, 0:1]), "SS")
                P.scalar(stats_s)

                def stats_r(v, tc=tc):
                    P.wg(v, "SS", P.cnt["SS"])
                    P.inc(v.reciprocal(rrow[0:1, :], rrow[0:1, :]), "VR")
                P.vector(stats_r)

                def rbc(t, tc=tc):
                    P.wg(t, "VR", P.cnt["VR"])
                    if tc >= 1:
                        P.wg(t, "HBD", 2 * tc)
                    P.inc(t.matmul(p_rb[:, :], onr[:, :], rrow[0:1, :],
                                   start=True, stop=True), "RBC")
                P.tensor(rbc)

                def norm(v, tc=tc, i=i):
                    P.wg(v, "RBC", P.cnt["RBC"])
                    for k in range(KT):
                        sl = sqf[:, k * TCW:(k + 1) * TCW]
                        P.inc(v.tensor_mul(sl, ut[i][:, k * TCW:(k + 1) * TCW], p_rb[:, :]), "VUe")
                        P.wg(v, "VUe")
                        ins = v.tensor_scalar_mul(hb[:, k * TCW:(k + 1) * TCW],
                                                  sl, prm[:, k:k + 1])
                        if k == KT - 1:
                            P.inc(ins, "HBD", 2)
                P.vector(norm)

                for ti, (wt, dst) in enumerate(((wq, qb), (wk, kb), (wv, vb))):
                    for mt in range(2):
                        n = tc * 6 + ti * 2 + mt
                        jb = n % 2

                        def qkv_mm(t, wt=wt, mt=mt, jb=jb, n=n, tc=tc):
                            if tc == 0 and n == 0:
                                P.wg(t, "Lw", 48)
                            P.wg(t, "HBD", P.cnt["HBD"])
                            if n >= 2:
                                P.wg(t, "QKE", n - 1)
                            for k in range(KT):
                                ins = t.matmul(p_a[jb][:, :],
                                               wt[:, k * 256 + mt * 128:k * 256 + (mt + 1) * 128],
                                               hb[:, k * TCW:(k + 1) * TCW],
                                               start=(k == 0), stop=(k == KT - 1))
                                if k == KT - 1:
                                    P.inc(ins, "QKM")
                        P.tensor(qkv_mm)

                        def qkv_ev(s_, dst=dst, mt=mt, jb=jb, tc=tc):
                            P.wg(s_, "QKM", P.cnt["QKM"])
                            if tc >= 1:
                                P.wg(s_, "CHD", P.cnt.get("CHD", 0))
                            P.inc(s_.copy(dst[:, mt * TCW:(mt + 1) * TCW], p_a[jb][:, :]), "QKE")
                        P.scalar(qkv_ev)

                for c4 in range(TCW // LC):
                    for hp in range(2):
                        co = c4 * LC

                        def pdec(v, hp=hp, co=co, tc=tc):
                            P.wg(v, "QKE", P.cnt["QKE"])
                            P.wg(v, "KDT", P.cnt.get("KDT", 0))
                            P.inc(v.tensor_mul(kdec[:, :], kb[:, hp * TCW + co: hp * TCW + co + LC],
                                               grv[:, hp * LC:(hp + 1) * LC]), "PDC")
                            P.inc(v.tensor_mul(qdec[:, :], qb[:, hp * TCW + co: hp * TCW + co + LC],
                                               gfw[:, hp * LC:(hp + 1) * LC]), "PDC")
                        P.vector(pdec)

                        def trs(t, hp=hp, co=co):
                            P.wg(t, "PDC", P.cnt["PDC"])
                            P.wg(t, "KTE", P.cnt.get("KTE", 0))
                            P.wg(t, "ATE", P.cnt.get("ATE", 0))
                            t.transpose(p_bt[:, 0:128], kdec[:, :], eye[:, :])
                            P.inc(t.transpose(p_bt[:, 128:256],
                                              vb[:, hp * TCW + co: hp * TCW + co + LC],
                                              eye[:, :]), "KTT")
                        P.tensor(trs)

                        def tre(s_):
                            P.wg(s_, "KTT", P.cnt["KTT"])
                            s_.copy(ktT[:, :], p_bt[:, 0:128])
                            P.inc(s_.copy(vtT[:, :], p_bt[:, 128:256]), "KTE")
                        P.scalar(tre)

                        for hh in range(2):
                            h = hp * 2 + hh
                            r0 = hh * 64
                            cg = (tc * 4 + c4) * 4 + h

                            def scm(t, hp=hp, r0=r0, co=co, cg=cg):
                                P.wg(t, "QKE", P.cnt["QKE"])
                                if cg >= 2:
                                    P.wg(t, "ATN", cg - 1)
                                jb2 = cg % 2
                                lhs = qb[r0:r0 + 64, hp * TCW + co: hp * TCW + co + LC]
                                rhs = kb[r0:r0 + 64, hp * TCW + co: hp * TCW + co + LC]
                                P.inc(t.matmul(p_a[jb2][:, 0:LC], lhs, rhs,
                                               start=True, stop=True), "SCM")
                            P.tensor(scm)

                            def msk(v, h=h, cg=cg):
                                P.wg(v, "SCM", P.cnt["SCM"])
                                P.wg(v, "ATT", P.cnt.get("ATT", 0))
                                jb2 = cg % 2
                                P.inc(v.tensor_mul(att[:, :], p_a[jb2][:, 0:LC],
                                                   dmk[:, h * LC:(h + 1) * LC]), "ATN")
                            P.vector(msk)

                            def atr(t, cg=cg):
                                P.wg(t, "ATN", P.cnt["ATN"])
                                P.wg(t, "ATE", P.cnt.get("ATE", 0))
                                P.wg(t, "KTE", P.cnt.get("KTE", 0))
                                P.inc(t.transpose(p_bt[:, 256:384], att[:, :], eye[:, :]), "ATT")
                            P.tensor(atr)

                            def ate(s_, cg=cg):
                                P.wg(s_, "ATT", P.cnt["ATT"])
                                P.inc(s_.copy(atT[:, :], p_bt[:, 256:384]), "ATE")
                            P.scalar(ate)

                            def omm(t, hp=hp, hh=hh, r0=r0, co=co, cg=cg):
                                P.wg(t, "ATE", P.cnt["ATE"])
                                P.wg(t, "KTE", P.cnt["KTE"])
                                P.wg(t, "SUP", P.cnt.get("SUP", 0))
                                if cg >= 1:
                                    P.wg(t, "OEV", P.cnt.get("OEV", 0))
                                t.matmul(p_o[r0:r0 + 64, :], vtT[:, hh * 64:(hh + 1) * 64],
                                         atT[:, :], start=True, stop=False)
                                P.inc(t.matmul(p_o[r0:r0 + 64, :],
                                               sesb[r0:r0 + 64, hp * 64:(hp + 1) * 64],
                                               qdec[r0:r0 + 64, :],
                                               start=False, stop=True), "OMM")
                                P.inc(t.matmul(p_st[r0:r0 + 64, 0:64],
                                               ktT[:, hh * 64:(hh + 1) * 64],
                                               vtT[:, hh * 64:(hh + 1) * 64],
                                               start=True, stop=True), "SMM")
                            P.tensor(omm)

                            def oev(s_, hp=hp, r0=r0, co=co, cg=cg, i=i, tc=tc):
                                P.wg(s_, "OMM", P.cnt["OMM"])
                                if tc >= 2 and cg % 16 == 0:
                                    P.wg(s_, f"YS{i}", 16 * (tc // 2))
                                P.inc(s_.copy(yb[i][r0:r0 + 64, hp * TCW + co: hp * TCW + co + LC],
                                              p_o[r0:r0 + 64, :]), "OEV")
                            P.scalar(oev)

                            def sup(v, hp=hp, r0=r0, cg=cg):
                                P.wg(v, "SMM", P.cnt["SMM"])
                                P.inc(v.scalar_tensor_tensor(
                                    sest[r0:r0 + 64, hp * 64:(hp + 1) * 64],
                                    sest[r0:r0 + 64, hp * 64:(hp + 1) * 64],
                                    glc[r0:r0 + 64, hp:hp + 1],
                                    p_st[r0:r0 + 64, 0:64], ALU.mult, ALU.add), "SUP")
                                P.wg(v, "SUP")
                                P.inc(v.tensor_copy(sesb[r0:r0 + 64, hp * 64:(hp + 1) * 64],
                                                    sest[r0:r0 + 64, hp * 64:(hp + 1) * 64]),
                                      "SUP")
                            P.vector(sup)
                def chd(s_, tc=tc):
                    P.wg(s_, "OEV", P.cnt["OEV"])
                    P.inc(s_.nop(), "CHD", 6)
                P.scalar(chd)

                def gstat(v, tc=tc, i=i):
                    P.wg(v, "OEV", P.cnt["OEV"])
                    if tc == 0:
                        P.wg(v, "ZI", 1)
                    for mt in range(2):
                        sl = yb[i][:, mt * TCW:(mt + 1) * TCW]
                        P.inc(v.reduce_sum(rrow[0:1, 0:1] if False else srow[0:1, 0:1], sl[0:1, 0:0], axis=AX.X), "NIL") if False else None
                    P.inc(v.engine_nop(), "GSD")
                P.vector(gstat) if False else None

                def gstat2(v, tc=tc, i=i):
                    P.wg(v, "OEV", P.cnt["OEV"])
                    if tc == 0:
                        P.wg(v, "ZI", 1)
                    if tc >= 1:
                        P.wg(v, "GSD", tc)
                    for mt in range(2):
                        sl = yb[i][:, mt * TCW:(mt + 1) * TCW]
                        P.wg(v, "GSa", P.cnt.get("GSa", 0))
                        P.inc(v.reduce_sum(sqf[:, 0:1], sl, axis=AX.X), "GSa")
                        P.wg(v, "GSa")
                        P.inc(v.tensor_add(gns[:, mt * 2:mt * 2 + 1],
                                           gns[:, mt * 2:mt * 2 + 1], sqf[:, 0:1]), "GSa")
                    P.inc(v.engine_nop(), "GSD")
                P.vector(gstat2)

                def gstat3(s_, tc=tc, i=i):
                    P.wg(s_, "GSD", P.cnt["GSD"])
                    for mt in range(2):
                        ins = s_.activation(sqf[:, mt * TCW:(mt + 1) * TCW],
                                            yb[i][:, mt * TCW:(mt + 1) * TCW], AF.Square)
                        if mt == 1:
                            P.inc(ins, "GSq")
                P.scalar(gstat3)

                def gstat4(v, tc=tc):
                    P.wg(v, "GSq", P.cnt["GSq"])
                    for mt in range(2):
                        P.wg(v, "GSb", P.cnt.get("GSb", 0))
                        P.inc(v.reduce_sum(sqf[:, TCW - 1:TCW] if False else sqf[:, 1:2],
                                           sqf[:, mt * TCW:(mt + 1) * TCW], axis=AX.X), "GSb")
                        P.wg(v, "GSb")
                        P.inc(v.tensor_add(gns[:, mt * 2 + 1:mt * 2 + 2],
                                           gns[:, mt * 2 + 1:mt * 2 + 2], sqf[:, 1:2]), "GSb")
                    P.inc(v.engine_nop(), "GSD")
                P.vector(gstat4)

                def ystore(g, tc=tc, i=i):
                    P.wg(g, "GSD", P.cnt["GSD"])
                    P.inc(g.dma_start(
                        out=yout.rearrange("(m p) t -> p m t", p=128)[:, :, tc * TCW:(tc + 1) * TCW],
                        in_=yb[i].rearrange("p (m t) -> p m t", m=2)), f"YS{i}", 16)
                P.gpsimd(ystore)

            def finmm(t):
                P.wg(t, "GSD", P.cnt["GSD"])
                for mt in range(2):
                    P.inc(t.matmul(p_st[0:2, mt * 2:mt * 2 + 2], indh[:, :],
                                   gns[:, mt * 2:mt * 2 + 2], start=True, stop=True), "FSM")
            P.tensor(finmm)

            def finev(v):
                P.wg(v, "FSM", P.cnt["FSM"])
                P.inc(v.tensor_copy(gno[:, 0:4], p_st[0:2, 0:4]), "FSE")
            P.vector(finev)

            def finst(g):
                P.wg(g, "FSE", 1)
                P.inc(g.dma_start(out=gnp_o[:, 0:4], in_=gno[:, 0:4]), "GPS", 16)
                P.wg(g, "GPS")
                P.wg(g, "YS0")
                P.wg(g, "YS1")
            P.gpsimd(finst)
        _prog_cache["pr1"] = nc
    return nc


def build_pr2():
    if "pr2" in _prog_cache:
        return _prog_cache["pr2"]
    nc = bass.Bass("TRN2", target_bir_lowering=False, debug=False)
    stack = contextlib.ExitStack()
    with stack:
        xin = nc.dram_tensor("xin", [D, T], F32, kind="ExternalInput").ap()
        uin = nc.dram_tensor("uin", [D, T], F32, kind="ExternalInput").ap()
        yall = nc.dram_tensor("yall", [D, T], F32, kind="ExternalInput").ap()
        gnp = nc.dram_tensor("gnp", [16, 2], F32, kind="ExternalInput").ap()
        ln1 = nc.dram_tensor("ln1", [128, KT], F32, kind="ExternalInput").ap()
        ln2 = nc.dram_tensor("ln2", [128, KT], F32, kind="ExternalInput").ap()
        gnw_d = nc.dram_tensor("gnw", [128, KT], F32, kind="ExternalInput").ap()
        gnb_d = nc.dram_tensor("gnb", [128, KT], F32, kind="ExternalInput").ap()
        wgT = nc.dram_tensor("wgT", [D, D], F32, kind="ExternalInput").ap()
        woT = nc.dram_tensor("woT", [D, D], F32, kind="ExternalInput").ap()
        gwT = nc.dram_tensor("gwT", [D, 2816], F32, kind="ExternalInput").ap()
        uwT = nc.dram_tensor("uwT", [D, 2816], F32, kind="ExternalInput").ap()
        dwT = nc.dram_tensor("dwT", [2816, D], F32, kind="ExternalInput").ap()
        indgn_d = nc.dram_tensor("indgn", [16, KT * 128], F32, kind="ExternalInput").ap()
        onc_d = nc.dram_tensor("onc", [128, 1], F32, kind="ExternalInput").ap()
        onr_d = nc.dram_tensor("onr", [1, 128], F32, kind="ExternalInput").ap()
        e6_d = nc.dram_tensor("e6", [1, 1], F32, kind="ExternalInput").ap()
        e5_d = nc.dram_tensor("e5", [16, 1], F32, kind="ExternalInput").ap()
        cnt_d = nc.dram_tensor("cntc", [16, 1], F32, kind="ExternalInput").ap()
        xout = nc.dram_tensor("xout", [D, T], F32, kind="ExternalOutput").ap()
        v2d = nc.dram_tensor("v2scr", [D, T], F32).ap()

        MT2 = 22
        h2 = sbt(nc, stack, "h2", [128, KT * T], BF16)
        zml = sbt(nc, stack, "zml", [128, MT2 * THW], BF16)
        dwr = sbt(nc, stack, "dwr", [128, MT2 * D], BF16)
        wgs = dwr.ap()[:, 0:KT * D] if False else None
        mws = [sbt(nc, stack, f"mws{i}", [128, KT * 256], BF16) for i in range(2)]
        ut = sbt(nc, stack, "ut", [128, KT * TCW], F32)
        sqf = sbt(nc, stack, "sqf", [128, (KT // 2) * TCW], F32)
        htc = sbt(nc, stack, "htc", [128, KT * TCW], BF16)
        gyb = sbt(nc, stack, "gyb", [128, KT * TCW], BF16)
        yrs = [sbt(nc, stack, f"yrs{i}", [128, TCW], F32) for i in range(2)]
        xrs = [sbt(nc, stack, f"xrs{i}", [128, TCW], F32) for i in range(2)]
        vrs = yrs
        gel = [sbt(nc, stack, f"gel{i}", [128, TCW], F32) for i in range(2)]
        uo = [sbt(nc, stack, f"uo{i}", [128, TCW], F32) for i in range(2)]
        prm = sbt(nc, stack, "prm", [128, 6 * KT], F32)
        gnr = sbt(nc, stack, "gnr", [16, 4], F32)
        indgn = sbt(nc, stack, "indgn_s", [16, KT * 128], F32)
        onc = sbt(nc, stack, "onc_s", [128, 1], F32)
        onr = sbt(nc, stack, "onr_s", [1, 128], F32)
        e6 = sbt(nc, stack, "e6_s", [1, 1], F32)
        e5 = sbt(nc, stack, "e5_s", [16, 1], F32)
        cntc = sbt(nc, stack, "cnt_s", [16, 1], F32)
        srow = sbt(nc, stack, "srow", [1, TCW], F32)
        p_q = pst(nc, stack, "p_q", [1, TCW])
        p_rb = pst(nc, stack, "p_rb", [128, TCW])
        p_a = [pst(nc, stack, f"p_a{i}", [128, TCW]) for i in range(2)]
        p_gu = [pst(nc, stack, f"p_gu{i}", [128, TCW]) for i in range(2)]

        L1O, L2O, GWO, GBO, MNO, RGO = 0, KT, 2 * KT, 3 * KT, 4 * KT, 5 * KT

        with nc.Block() as block:
            P = Em(nc, block, stack)

            def consts(q_):
                for dst, srcd, w in ((prm[:, L1O:L1O + KT], ln1, None),
                                     (prm[:, L2O:L2O + KT], ln2, None),
                                     (prm[:, GWO:GWO + KT], gnw_d, None),
                                     (prm[:, GBO:GBO + KT], gnb_d, None),
                                     (indgn[:, :], indgn_d, None),
                                     (onc[:, :], onc_d, None),
                                     (onr[:, :], onr_d, None),
                                     (e6[:, :], e6_d, None),
                                     (e5[:, :], e5_d, None),
                                     (cntc[:, :], cnt_d, None),
                                     (gnr[:, 0:2], gnp, None)):
                    P.inc(q_.dma_start(out=dst, in_=srcd[:, :]), "Lc", 16)
            P.sync(consts)
            n_const = P.cnt["Lc"]

            wgs = dwr[:, 0:KT * D]
            wos = dwr[:, KT * D:2 * KT * D]

            def wcast(g):
                P.inc(g.dma_start(out=dwr.rearrange("p (k m) -> p k m", k=MT2)[:, 0:KT],
                                  in_=wgT.rearrange("(k p) m -> p k m", p=128)), "Lw", 16)
                P.inc(g.dma_start(out=dwr.rearrange("p (k m) -> p k m", k=MT2)[:, KT:2 * KT],
                                  in_=woT.rearrange("(k p) m -> p k m", p=128)), "Lw", 16)
            P.gpsimd(wcast)

            def dwload(g):
                P.wg(g, "WOM", 32)
                P.inc(g.dma_start(out=dwr.rearrange("p (k m) -> p k m", k=MT2),
                                  in_=dwT.rearrange("(k p) m -> p k m", p=128)), "Ldw", 16)
            dwload_fn = dwload

            def gmath_v(v):
                P.wg(v, "Lc", n_const)
                P.inc(v.tensor_scalar_mul(gnr[:, 2:3], gnr[:, 0:1], cntc[:, 0:1]), "GM")
                P.wg(v, "GM")
                P.inc(v.tensor_mul(gnr[:, 3:4], gnr[:, 2:3], gnr[:, 2:3]), "GM")
                P.wg(v, "GM")
                P.inc(v.scalar_tensor_tensor(gnr[:, 3:4], gnr[:, 1:2], cntc[:, 0:1],
                                             gnr[:, 3:4], ALU.mult, ALU.subtract), "GM")
            P.vector(gmath_v)

            def gmath_s(s_):
                P.wg(s_, "GM", P.cnt["GM"])
                P.inc(s_.activation(gnr[:, 3:4], gnr[:, 3:4], AF.Sqrt, bias=e5[:, 0:1]), "GS")
            P.scalar(gmath_s)

            def gmath_r(v):
                P.wg(v, "GS", P.cnt["GS"])
                P.inc(v.reciprocal(gnr[:, 3:4], gnr[:, 3:4]), "GR")
            P.vector(gmath_r)

            for k in range(KT):
                def gbc(t, k=k):
                    P.wg(t, "GR", 1)
                    if k >= 1:
                        P.wg(t, "GBE", k)
                    P.inc(t.matmul(p_rb[:, 0:2], indgn[:, k * 128:(k + 1) * 128],
                                   gnr[:, 2:4], start=True, stop=True), "GBM")
                P.tensor(gbc)

                def gbe(v, k=k):
                    P.wg(v, "GBM", P.cnt["GBM"])
                    v.tensor_copy(prm[:, MNO + k:MNO + k + 1], p_rb[:, 0:1])
                    P.inc(v.tensor_copy(prm[:, RGO + k:RGO + k + 1], p_rb[:, 1:2]), "GBE")
                P.vector(gbe)
            def rgm(v):
                P.wg(v, "GBE", P.cnt["GBE"])
                P.inc(v.tensor_mul(prm[:, RGO:RGO + KT], prm[:, RGO:RGO + KT],
                                   prm[:, GWO:GWO + KT]), "RGD")
            P.vector(rgm)

            def rms_tc(tc, src_ap, lnoff, hdst, dsem, extra_guard=None):
                def load(q_, tc=tc):
                    P.wg(q_, "HBD", P.cnt.get("HBD", 0))
                    if extra_guard is not None:
                        P.wg(q_, extra_guard, P.cnt.get(extra_guard, 0))
                    P.inc(q_.dma_start(out=ut.rearrange("p (k t) -> p k t", k=KT),
                                       in_=src_ap[:, tc * TCW:(tc + 1) * TCW]
                                       .rearrange("(k p) t -> p k t", p=128)
                                       if False else
                                       kpt(src_ap)[:, :, tc * TCW:(tc + 1) * TCW]), dsem, 16)
                P.sync(load)

                for half in range(2):
                    def squares(s_, tc=tc, half=half):
                        if half == 0:
                            P.wg(s_, dsem, P.cnt[dsem])
                            P.wg(s_, "HBD", P.cnt.get("HBD", 0))
                        else:
                            P.wg(s_, "MQh", P.cnt.get("MQh", 0))
                        if half == 0:
                            P.wg(s_, "YND", P.cnt.get("YND", 0))
                        for k2 in range(KT // 2):
                            ins = s_.activation(sqf[:, k2 * TCW:(k2 + 1) * TCW],
                                                ut[:, (half * 4 + k2) * TCW:(half * 4 + k2 + 1) * TCW],
                                                AF.Square)
                            if k2 == KT // 2 - 1:
                                P.inc(ins, "SQ")
                    P.scalar(squares)

                    def smm(t, tc=tc, half=half):
                        if P.cnt.get("MQ", 0) == 0 and half == 0:
                            P.wg(t, "Lc", n_const)
                        P.wg(t, "SQ", P.cnt["SQ"])
                        if half == 0:
                            P.wg(t, "VM", P.cnt.get("VM", 0))
                        for k2 in range(KT // 2):
                            ins = t.matmul(p_q[:, :], onc[:, :], sqf[:, k2 * TCW:(k2 + 1) * TCW],
                                           start=(half == 0 and k2 == 0),
                                           stop=(half == 1 and k2 == KT // 2 - 1))
                            if k2 == KT // 2 - 1:
                                P.inc(ins, "MQh" if half == 0 else "MQ")
                    P.tensor(smm)

                def sv(v):
                    P.wg(v, "MQ", P.cnt["MQ"])
                    P.inc(v.tensor_scalar_mul(srow[0:1, :], p_q[0:1, :], 1.0 / D), "VM")
                P.vector(sv)

                def ss(s_):
                    P.wg(s_, "VM", P.cnt["VM"])
                    P.inc(s_.activation(srow[0:1, :], srow[0:1, :], AF.Sqrt,
                                        bias=e6[0:1, 0:1]), "SS")
                P.scalar(ss)

                def sr(v):
                    P.wg(v, "SS", P.cnt["SS"])
                    P.inc(v.reciprocal(srow[0:1, :], srow[0:1, :]), "VR")
                P.vector(sr)

                def rbc(t):
                    P.wg(t, "VR", P.cnt["VR"])
                    P.wg(t, "HBD", P.cnt.get("HBD", 0))
                    P.inc(t.matmul(p_rb[:, :], onr[:, :], srow[0:1, :],
                                   start=True, stop=True), "RBC")
                P.tensor(rbc)

                def norm(v, tc=tc):
                    P.wg(v, "RBC", P.cnt["RBC"])
                    for k in range(KT):
                        sl = sqf[:, (k % 4) * TCW:((k % 4) + 1) * TCW]
                        P.inc(v.tensor_mul(sl, ut[:, k * TCW:(k + 1) * TCW], p_rb[:, :]), "VUe")
                        P.wg(v, "VUe")
                        ins = v.tensor_scalar_mul(hdst[:, k * TCW:(k + 1) * TCW]
                                                  if hdst is htc else
                                                  hdst[:, k * T + tc * TCW:k * T + (tc + 1) * TCW],
                                                  sl, prm[:, lnoff + k:lnoff + k + 1])
                        if k == KT - 1:
                            P.inc(ins, "HBD")
                P.vector(norm)

            for tc in range(TC):
                rms_tc(tc, uin, L1O, htc, "Lu", extra_guard="V2E")
                for mo in range(KT):
                    n = tc * KT + mo
                    jb = n % 2

                    def g_mm(t, mo=mo, jb=jb, n=n):
                        if n == 0:
                            P.wg(t, "Lw", 32)
                        P.wg(t, "HBD", P.cnt["HBD"])
                        if n >= 2:
                            P.wg(t, "YND", n - 1)
                        for k in range(KT):
                            ins = t.matmul(p_a[jb][:, :],
                                           wgs[:, k * D + mo * 128:k * D + (mo + 1) * 128],
                                           htc[:, k * TCW:(k + 1) * TCW],
                                           start=(k == 0), stop=(k == KT - 1))
                            if k == KT - 1:
                                P.inc(ins, "GMM")
                    P.tensor(g_mm)

                    def yload(q_, mo=mo, tc=tc, n=n, jb=jb):
                        if n >= 2:
                            P.wg(q_, "YND", n - 1)
                        P.inc(q_.dma_start(
                            out=yrs[jb][:, :],
                            in_=yall[mo * 128:(mo + 1) * 128, tc * TCW:(tc + 1) * TCW]),
                            f"Ly{jb}", 16)
                    P.sync(yload)

                    def g_ev(s_, jb=jb, n=n):
                        P.wg(s_, "GMM", P.cnt["GMM"])
                        P.wg(s_, "YND", P.cnt.get("YND", 0))
                        P.inc(s_.activation(sqf[:, 0:TCW], p_a[jb][:, :], AF.Sigmoid), "GEV")
                    P.scalar(g_ev)

                    def yn_gy(v, mo=mo, jb=jb, n=n):
                        P.wg(v, "GEV", P.cnt["GEV"])
                        P.wg(v, f"Ly{jb}", P.cnt[f"Ly{jb}"])
                        P.wg(v, "RGD", 1)
                        P.inc(v.tensor_scalar_sub(yrs[jb][:, :], yrs[jb][:, :],
                                                  prm[:, MNO + mo:MNO + mo + 1]), "YNa")
                        P.wg(v, "YNa")
                        P.inc(v.tensor_scalar(yrs[jb][:, :], yrs[jb][:, :],
                                              prm[:, RGO + mo:RGO + mo + 1],
                                              prm[:, GBO + mo:GBO + mo + 1],
                                              ALU.mult, ALU.add), "YNa")
                        P.wg(v, "YNa")
                        P.inc(v.tensor_mul(gyb[:, mo * TCW:(mo + 1) * TCW],
                                           yrs[jb][:, :], sqf[:, 0:TCW]), "YND")
                    P.vector(yn_gy)

                for mo in range(KT):
                    n2 = tc * KT + mo
                    jb = n2 % 2

                    def wo_mm(t, mo=mo, jb=jb, n2=n2):
                        P.wg(t, "YND", P.cnt["YND"])
                        if n2 >= 2:
                            P.wg(t, "V2E", n2 - 1)
                        for k in range(KT):
                            ins = t.matmul(p_a[jb][:, :],
                                           wos[:, k * D + mo * 128:k * D + (mo + 1) * 128],
                                           gyb[:, k * TCW:(k + 1) * TCW],
                                           start=(k == 0), stop=(k == KT - 1))
                            if k == KT - 1:
                                P.inc(ins, "WOM")
                    P.tensor(wo_mm)

                    def v2ev(v, mo=mo, jb=jb, n2=n2):
                        P.wg(v, "WOM", P.cnt["WOM"])
                        if n2 >= 2:
                            P.wg(v, f"V2S{jb}", 16 * (n2 // 2))
                        P.inc(v.tensor_add(uo[jb][:, :], p_a[jb][:, :],
                                           ut[mo * 128 - mo * 128:128, mo * TCW:(mo + 1) * TCW]
                                           if False else ut[:, mo * TCW:(mo + 1) * TCW]), "V2E")
                    P.vector(v2ev)

                    def v2st(g, mo=mo, tc=tc, jb=jb):
                        P.wg(g, "V2E", P.cnt["V2E"])
                        P.inc(g.dma_start(
                            out=v2d[mo * 128:(mo + 1) * 128, tc * TCW:(tc + 1) * TCW],
                            in_=uo[jb][:, :]), f"V2S{jb}", 16)
                    P.gpsimd(v2st)

            P.gpsimd(dwload_fn)

            for tc in range(TC):
                def wv2(q_, tc=tc):
                    P.wg(q_, "V2S0", P.cnt["V2S0"])
                    P.wg(q_, "V2S1", P.cnt["V2S1"])
                P.sync(wv2) if tc == 0 else None
                rms_tc(tc, v2d, L2O, h2, "Lv")

            n2 = 0
            n3 = 0
            for th in range(NTH):
                for mp in range(MT2):
                    i = mp % 2

                    def mwl(g, th=th, mp=mp, i=i):
                        gidx = th * MT2 + mp
                        if gidx >= 2:
                            P.wg(g, "ZD2", 2 * (gidx - 1))
                        v = mws[i].rearrange("p (k m) -> p k m", k=KT)
                        P.inc(g.dma_start(
                            out=v[:, :, 0:128],
                            in_=gwT.rearrange("(k p) m -> p k m", p=128)[:, :, mp * 128:(mp + 1) * 128]),
                            f"Lm{i}", 16)
                        P.inc(g.dma_start(
                            out=v[:, :, 128:256],
                            in_=uwT.rearrange("(k p) m -> p k m", p=128)[:, :, mp * 128:(mp + 1) * 128]),
                            f"Lm{i}", 16)
                    P.gpsimd(mwl)

                    for tc2 in range(2):
                        j = n2 % 2
                        tof = th * THW + tc2 * TCW

                        def gu_mm(t, mp=mp, i=i, j=j, n2=n2, tof=tof, th=th):
                            if n2 == 0:
                                P.wg(t, "HBD", P.cnt["HBD"])
                            P.wg(t, f"Lm{i}", P.cnt[f"Lm{i}"])
                            if n2 >= 2:
                                P.wg(t, "ZD2", n2 - 1)
                            wvv = mws[i].rearrange("p (k m) -> p k m", k=KT)
                            for k in range(KT):
                                rhs = h2[:, k * T + tof: k * T + tof + TCW]
                                t.matmul(p_gu[j][:, :], wvv[:, k, 0:128], rhs,
                                         start=(k == 0), stop=(k == KT - 1))
                            for k in range(KT):
                                rhs = h2[:, k * T + tof: k * T + tof + TCW]
                                ins = t.matmul(p_a[j][:, :], wvv[:, k, 128:256], rhs,
                                               start=(k == 0), stop=(k == KT - 1))
                                if k == KT - 1:
                                    P.inc(ins, "GUM")
                        P.tensor(gu_mm)

                        def gsil(s_, j=j, n2=n2):
                            P.wg(s_, "GUM", P.cnt["GUM"])
                            if n2 >= 2:
                                P.wg(s_, "ZD2", n2 - 1)
                            P.inc(s_.activation(gel[j][:, :], p_gu[j][:, :], AF.Silu), "GSL")
                        P.scalar(gsil)

                        def zp(v, mp=mp, tc2=tc2, j=j):
                            P.wg(v, "GSL", P.cnt["GSL"])
                            P.inc(v.tensor_mul(
                                zml[:, mp * THW + tc2 * TCW: mp * THW + (tc2 + 1) * TCW],
                                gel[j][:, :], p_a[j][:, :]), "ZD2")
                        P.vector(zp)
                        n2 += 1

                for mo in range(KT):
                    for tc2 in range(2):
                        j = n3 % 2
                        tof = th * THW + tc2 * TCW
                        tcg = tof // TCW

                        def fload(q_, mo=mo, j=j, n3=n3, tof=tof, tcg=tcg):
                            if n3 >= 2:
                                P.wg(q_, "EVF", 2 * (n3 - 1))
                            P.inc(q_.dma_start(
                                out=xrs[j][:, :],
                                in_=xin[mo * 128:(mo + 1) * 128, tof: tof + TCW]), f"Lf{j}", 16)
                            P.inc(q_.dma_start(
                                out=vrs[j][:, :],
                                in_=v2d[mo * 128:(mo + 1) * 128, tof: tof + TCW]), f"Lf{j}", 16)
                        P.sync(fload)

                        def d_mm(t, mo=mo, tc2=tc2, j=j, n3=n3, th=th):
                            if n3 == 0:
                                P.wg(t, "ZD2", P.cnt["ZD2"])
                                P.wg(t, "Ldw", 16)
                            if th == 1 and n3 == 16:
                                P.wg(t, "ZD2", P.cnt["ZD2"])
                            if n3 >= 2:
                                P.wg(t, "EVF", 2 * (n3 - 1))
                            wvv = dwr.rearrange("p (k m) -> p k m", k=MT2)
                            for k in range(MT2):
                                ins = t.matmul(p_gu[j][:, :], wvv[:, k, mo * 128:(mo + 1) * 128],
                                               zml[:, k * THW + tc2 * TCW: k * THW + (tc2 + 1) * TCW],
                                               start=(k == 0), stop=(k == MT2 - 1))
                                if k == MT2 - 1:
                                    P.inc(ins, "FIM")
                        P.tensor(d_mm)

                        def fev(v, j=j, n3=n3):
                            P.wg(v, "FIM", P.cnt["FIM"])
                            P.wg(v, f"Lf{j}", P.cnt[f"Lf{j}"])
                            if n3 >= 2:
                                P.wg(v, f"SO{j}", 16 * (n3 // 2))
                            P.inc(v.tensor_add(uo[j][:, :], p_gu[j][:, :], vrs[j][:, :]), "EVF")
                            P.wg(v, "EVF")
                            P.inc(v.tensor_add(uo[j][:, :], uo[j][:, :], xrs[j][:, :]), "EVF")
                        P.vector(fev)

                        def fst(g, mo=mo, j=j, tof=tof):
                            P.wg(g, "EVF", P.cnt["EVF"])
                            P.inc(g.dma_start(
                                out=xout[mo * 128:(mo + 1) * 128, tof: tof + TCW],
                                in_=uo[j][:, :]), f"SO{j}", 16)
                        P.gpsimd(fst)
                        n3 += 1

            P.gpsimd(lambda g: (P.wg(g, "SO0"), P.wg(g, "SO1")))
        _prog_cache["pr2"] = nc
    return nc


def retnet_maps(ins):
    Wq = np.asarray(ins["L1_attn_Wq"], np.float32)
    Wk = np.asarray(ins["L1_attn_Wk"], np.float32)
    Wv = np.asarray(ins["L1_attn_Wv"], np.float32)
    gam = np.asarray(ins["L1_attn_gamma"], np.float32)
    ii = np.arange(LC)[:, None]
    jj = np.arange(LC)[None, :]
    ptile = np.where(jj > ii, 1e9, (ii - jj).astype(np.float64)).astype(np.float32)
    iof = np.concatenate([np.arange(1, LC + 1), LC - 1 - np.arange(LC), [LC]]).astype(np.float32)[None, :]
    indh = np.zeros((128, 2), np.float32)
    indh[0:64, 0] = 1.0
    indh[64:128, 1] = 1.0
    _, _, _, eye = pa_inds()
    m1 = []
    for j in range(4):
        sl = slice(j * 256, (j + 1) * 256)
        m1.append(dict(wqT=np.ascontiguousarray(Wq[sl].T), wkT=np.ascontiguousarray(Wk[sl].T),
                       wvT=np.ascontiguousarray(Wv[sl].T),
                       ln1=per_part(ins["L1_attn_ln1_w"]),
                       gam=np.ascontiguousarray(gam[4 * j:4 * j + 4][None, :]),
                       ptile=ptile, iof=np.ascontiguousarray(iof),
                       onr=ONES_ROW, onc=ONES_COL,
                       eps=np.full((1, 1), 1e-6, np.float32), eye=eye, indh=indh))
    indgn = np.zeros((16, KT * 128), np.float32)
    for k in range(KT):
        for p in range(128):
            indgn[2 * k + p // 64, k * 128 + p] = 1.0
    m2 = dict(ln1=per_part(ins["L1_attn_ln1_w"]), ln2=per_part(ins["L1_attn_ln2_w"]),
              gnw=per_part(ins["L1_attn_gn_w"]), gnb=per_part(ins["L1_attn_gn_b"]),
              wgT=np.ascontiguousarray(np.asarray(ins["L1_attn_Wg"], np.float32).T),
              woT=np.ascontiguousarray(np.asarray(ins["L1_attn_Wo"], np.float32).T),
              gwT=np.ascontiguousarray(np.asarray(ins["L1_attn_gate_w"], np.float32).T),
              uwT=np.ascontiguousarray(np.asarray(ins["L1_attn_up_w"], np.float32).T),
              dwT=np.ascontiguousarray(np.asarray(ins["L1_attn_down_w"], np.float32).T),
              indgn=indgn, onc=ONES_COL, onr=ONES_ROW,
              e6=np.full((1, 1), 1e-6, np.float32),
              e5=np.full((16, 1), 1e-5, np.float32),
              cntc=np.full((16, 1), 1.0 / (64.0 * SEQ), np.float32))
    return m1, m2


def gnp_decode(gnp_raw):
    out = np.empty((4, 2), np.float32)
    for mt in range(2):
        for r in range(2):
            out[2 * mt + r, 0] = gnp_raw[r, mt * 2 + 0]
            out[2 * mt + r, 1] = gnp_raw[r, mt * 2 + 1]
    return out


def _run_mamba(x_cores, ins):
    u_cores = run_pln(x_cores)
    m1, m2, m3 = mamba_weight_maps(ins)
    u_full = [np.ascontiguousarray(np.concatenate(
        [np.asarray(u_cores[b * NB + s], np.float32) for s in range(NB)], axis=1))
        for b in range(B)]
    nc1 = build_pm1()
    im = []
    for c in range(NCORES):
        b, j = divmod(c, NB)
        im.append(dict(uin=u_full[b], **m1[j]))
    r1 = run_bass_kernel_spmd(nc1, im, core_ids=list(range(NCORES)))
    nc2 = build_pm2()
    im = []
    for c in range(NCORES):
        b, j = divmod(c, NB)
        mm = dict(xcs=r1.results[c]["xcs"], zo=r1.results[c]["zo"], **m2[j])
        for r in range(4):
            mm[f"pd{r}"] = r1.results[b * NB + r]["pdbl"]
        im.append(mm)
    r2 = run_bass_kernel_spmd(nc2, im, core_ids=list(range(NCORES)))
    nc3 = build_pm3()
    im = []
    for c in range(NCORES):
        b, s = divmod(c, NB)
        gfull = np.concatenate([r2.results[b * NB + j]["go"] for j in range(NB)], axis=0)
        im.append(dict(xin=np.ascontiguousarray(np.asarray(x_cores[c], np.float32)),
                       uin=np.ascontiguousarray(np.asarray(u_cores[c], np.float32)),
                       gall=np.ascontiguousarray(gfull[:, s * T:(s + 1) * T]), **m3))
    r3 = run_bass_kernel_spmd(nc3, im, core_ids=list(range(NCORES)))
    return [r3.results[c]["xout"] for c in range(NCORES)]


def _run_retnet(x_cores, ins):
    u_cores = run_pln(x_cores)
    m1, m2 = retnet_maps(ins)
    u_full = [np.ascontiguousarray(np.concatenate(
        [np.asarray(u_cores[b * NB + s], np.float32) for s in range(NB)], axis=1))
        for b in range(B)]
    nc1 = build_pr1()
    im = []
    for c in range(NCORES):
        b, j = divmod(c, NB)
        im.append(dict(uin=u_full[b], **m1[j]))
    r1 = run_bass_kernel_spmd(nc1, im, core_ids=list(range(NCORES)))
    nc2 = build_pr2()
    im = []
    for c in range(NCORES):
        b, s = divmod(c, NB)
        yfull = np.concatenate([r1.results[b * NB + j]["yout"] for j in range(NB)], axis=0)
        gnp = np.concatenate([gnp_decode(r1.results[b * NB + j]["gnp"]) for j in range(NB)], axis=0)
        im.append(dict(xin=np.ascontiguousarray(np.asarray(x_cores[c], np.float32)),
                       uin=np.ascontiguousarray(np.asarray(u_cores[c], np.float32)),
                       yall=np.ascontiguousarray(yfull[:, s * T:(s + 1) * T]),
                       gnp=np.ascontiguousarray(gnp), **m2))
    r2 = run_bass_kernel_spmd(nc2, im, core_ids=list(range(NCORES)))
    return [r2.results[c]["xout"] for c in range(NCORES)]


def _run_l0ffn(x_cores, ins):
    u_cores = run_pln(x_cores)
    for i in range(6):
        u_cores = run_pa(u_cores, pa_weights(ins["L0_ffn_Wqkv"][i], ins["L0_ffn_Wo"][i]),
                         ins["L0_ffn_q_scale"][i], ins["L0_ffn_k_scale"][i])
        wmap = pf_weights(ins["L0_ffn_W1"][i], ins["L0_ffn_b1"][i], ins["L0_ffn_W2"][i],
                          ins["L0_ffn_ln_g"][i], ins["L0_ffn_ln_b"][i], ins["L0_ffn_b2"][i])
        u_cores = run_pf(u_cores, wmap, last=(i == 5), x_cores=(x_cores if i == 5 else None))
    return u_cores


def _run_moe(x_cores, ins):
    u_cores = run_pln(x_cores)
    return run_pmoe(u_cores, x_cores, pmoe_inputs(ins))


def kernel(**inputs):
    ins = {k: np.asarray(v) for k, v in inputs.items()}
    x_cores = to_fm(ins["L0_attn_x"])
    x_cores = _run_l0ffn(x_cores, ins)
    x_cores = _run_mamba(x_cores, ins)
    x_cores = _run_l0ffn(x_cores, ins)
    x_cores = _run_moe(x_cores, ins)
    x_cores = _run_retnet(x_cores, ins)
    x_cores = _run_moe(x_cores, ins)
    return from_fm(x_cores)
